# version 01 (final)
# baseline (speedup 1.0000x reference)
import math
import numpy as np
import ml_dtypes
from contextlib import ExitStack

import concourse.bass as bass
import concourse.mybir as mybir
from concourse.bass_utils import run_bass_kernel_spmd
from concourse.bass_core_rust import IndirectOffsetOnAxis

F32 = mybir.dt.float32
BF16 = mybir.dt.bfloat16
I32 = mybir.dt.int32
AF = mybir.ActivationFunctionType
ALU = mybir.AluOpType
AX = mybir.AxisListType
BF = ml_dtypes.bfloat16

N_CORES = 8
B = 2
S = 8192
D = 1024
T = 2048
NH = 16
HD = 64
EPS_LN = 1e-5
NBT = S // 128
NMT = T // 128

ENGINES = ("sp", "pe", "dve", "act", "pool")


class Sched:

    AUTO_CHAIN = ("dve", "act", "pool")

    def __init__(self):
        self.ops = {e: [] for e in ENGINES}
        self.cnt = {}
        self.last = {}

    def _bump(self, sem, amt):
        self.cnt[sem] = self.cnt.get(sem, 0) + amt
        return (sem, self.cnt[sem])

    def do(self, eng, fn, wait=(), tok=False, dma=False, sem=None):
        token = None
        inc = None
        if eng in self.AUTO_CHAIN and not dma:
            wait = tuple(wait) + (self.last.get(eng),)
            tok = True
        wait = tuple(w for w in wait if w is not None)
        if tok:
            s = sem or (("d_" + eng) if dma else ("s_" + eng))
            amt = 16 if dma else 1
            token = self._bump(s, amt)
            inc = (s, amt)
        self.ops[eng].append((fn, wait, inc))
        if eng in self.AUTO_CHAIN and not dma:
            self.last[eng] = token
        return token

    def emit(self, nc, final_waits=()):
        with ExitStack() as stx:
            sems = {name: stx.enter_context(nc.semaphore(name)) for name in self.cnt}
            hw = {e: {} for e in ENGINES}

            def mk(engname):
                def run(e):
                    for fn, waits, inc in self.ops[engname]:
                        for (s, v) in waits:
                            if hw[engname].get(s, 0) < v:
                                e.wait_ge(sems[s], v)
                                hw[engname][s] = v
                        inst = fn(e)
                        if inc is not None:
                            s, amt = inc
                            if inst is None:
                                inst = e.nop()
                            inst.then_inc(sems[s], amt)
                    if engname == "sp":
                        for (s, v) in final_waits:
                            if hw[engname].get(s, 0) < v:
                                e.wait_ge(sems[s], v)
                                hw[engname][s] = v
                return run

            with nc.Block() as block:
                block.sync(mk("sp"))
                block.tensor(mk("pe"))
                block.vector(mk("dve"))
                block.scalar(mk("act"))
                block.gpsimd(mk("pool"))


def new_nc(name):
    return bass.Bass("TRN2", target_bir_lowering=False, debug=False, name=name)



_DEBUG = False


def build_attn_l0(variant):
    assert variant in ("l0a", "l0b")
    rope = (variant == "l0b")
    nc = new_nc(variant)
    x_my = nc.dram_tensor("x_my", [T, D], F32, kind="ExternalInput")
    x_b = nc.dram_tensor("x_b", [S, D], F32, kind="ExternalInput")
    wq_d = nc.dram_tensor("wq", [D, D], F32, kind="ExternalInput")
    wk_d = nc.dram_tensor("wk", [D, D], F32, kind="ExternalInput")
    wv_d = nc.dram_tensor("wv", [D, D], F32, kind="ExternalInput")
    wp_d = nc.dram_tensor("wp", [D, D], F32, kind="ExternalInput")
    if rope:
        lng_d = nc.dram_tensor("lng", [1, D], F32, kind="ExternalInput")
        lnb_d = nc.dram_tensor("lnb", [1, D], F32, kind="ExternalInput")
        bout_d = nc.dram_tensor("bout", [1, D], F32, kind="ExternalInput")
        cosr_d = nc.dram_tensor("cosr", [T, 512], F32, kind="ExternalInput")
        sinr_d = nc.dram_tensor("sinr", [T, 512], F32, kind="ExternalInput")
        cosb_d = nc.dram_tensor("cosb", [S, 512], F32, kind="ExternalInput")
        sinb_d = nc.dram_tensor("sinb", [S, 512], F32, kind="ExternalInput")
    xo = nc.dram_tensor("xo", [T, D], F32, kind="ExternalOutput")

    kT_d = nc.dram_tensor("kT_d", [D, S], BF16)
    vT_d = nc.dram_tensor("vT_d", [D, S], BF16)
    if _DEBUG:
        dbg_k = nc.dram_tensor("dbg_k", [D, S], BF16, kind="ExternalOutput")
        dbg_v = nc.dram_tensor("dbg_v", [D, S], BF16, kind="ExternalOutput")
        dbg_nk = nc.dram_tensor("dbg_nk", [1, 16], F32, kind="ExternalOutput")
        dbg_q0 = nc.dram_tensor("dbg_q0", [65, 2048], BF16, kind="ExternalOutput")
        dbg_q7 = nc.dram_tensor("dbg_q7", [65, 2048], BF16, kind="ExternalOutput")
        dbg_aT = nc.dram_tensor("dbg_aT", [128, 8 * 2048], BF16, kind="ExternalOutput")

    i128c = nc.inline_tensor(np.eye(128, dtype=BF), name="ident128")
    i64c = nc.inline_tensor(np.eye(64, dtype=BF), name="ident64")
    i16c = nc.inline_tensor(np.eye(16, dtype=BF), name="ident16")
    ones1c = nc.inline_tensor(np.ones((1, 128), dtype=BF), name="ones1x128")
    ones1fc = nc.inline_tensor(np.ones((1, 128), dtype=np.float32), name="ones1x128f")

    S_ = Sched()
    st = ExitStack()
    with st:
        def sb(stk, name, shape, dt):
            return stk.enter_context(nc.sbuf_tensor(name, shape, dt))

        idn = sb(st, "idn", [128, 128], BF16)
        idn64 = sb(st, "idn64", [64, 64], BF16)
        idn16 = sb(st, "idn16", [16, 16], BF16)
        onesr = sb(st, "onesr", [1, 128], BF16)
        onesrf = sb(st, "onesrf", [1, 128], F32)
        wA = sb(st, "wA", [128, 8 * 1024], BF16)
        wB = sb(st, "wB", [128, 8 * 1024], BF16)
        xt = [sb(st, f"xt{i}", [128, 1024], F32) for i in range(2)]
        stats = [sb(st, f"stats{i}", [128, 16], F32) for i in range(2)]
        statc = sb(st, "statc", [128, 8 * NMT], F32)
        sqln = sb(st, "sqln", [128, 1024], BF16)
        sqkv = sb(st, "sqkv", [128, 1024], BF16)
        n2t = sb(st, "n2t", [128, 16], F32)
        n2tb = sb(st, "n2tb", [128, 16], BF16)
        nm16 = sb(st, "nm16", [16, 1], BF16)
        nkmax = sb(st, "nkmax", [1, 16], F32)
        nkb = sb(st, "nkb", [1, 16], BF16)
        mkb = sb(st, "mkb", [128, 16], F32)
        ctile = sb(st, "ctile", [128, 16], BF16)
        aT = sb(st, "aT", [128, 8 * 2048], BF16)
        if rope:
            brows = sb(st, "brows", [1, 4 * 1024], BF16)
            gcol = sb(st, "gcol", [128, 8], F32)
            betac = sb(st, "betac", [128, 8], BF16)

        q_stack = ExitStack()
        qtil = [sb(q_stack, f"qtil{h}", [65, 2048], BF16) for h in range(NH)]

        p1 = ExitStack()
        xhat = [sb(p1, f"xhat{i}", [128, 1024], BF16) for i in range(2)]
        xhTt = [sb(p1, f"xhTt{i}", [128, 1024], BF16) for i in range(2)]
        kvK = sb(p1, "kvK", [128, 1024], BF16)
        kvV = sb(p1, "kvV", [128, 1024], BF16)
        tstg = [sb(p1, f"tstg{i}", [128, 1024], BF16) for i in range(2)]
        if rope:
            ropes = sb(p1, "ropes", [128, 4 * 512], F32)
            rsc = [sb(p1, f"rsc{i}", [128, 512], BF16) for i in range(2)]

        psum_tr = st.enter_context(nc.psum_tensor("psum_tr", [128, 1024], BF16))
        ps1 = ExitStack()
        psum_p = ps1.enter_context(nc.psum_tensor("psum_p", [128, 1024], F32))
        psum_h16 = ps1.enter_context(nc.psum_tensor("psum_h16", [16, 128], BF16))
        psum_h1 = ps1.enter_context(nc.psum_tensor("psum_h1", [1, 128], BF16))
        psum_b = ps1.enter_context(nc.psum_tensor("psum_bx", [128, 1024], F32))

        t_id = S_.do("sp", lambda e: e.dma_start(out=idn[:], in_=i128c.ap()), tok=True, dma=True)
        t_id64 = S_.do("sp", lambda e: e.dma_start(out=idn64[:], in_=i64c.ap()), tok=True, dma=True)
        t_id16 = S_.do("sp", lambda e: e.dma_start(out=idn16[:], in_=i16c.ap()), tok=True, dma=True)
        t_or = S_.do("sp", lambda e: e.dma_start(out=onesr[:], in_=ones1c.ap()), tok=True, dma=True)
        t_orf = S_.do("sp", lambda e: e.dma_start(out=onesrf[:], in_=ones1fc.ap()), tok=True, dma=True)
        t_nk0 = S_.do("dve", lambda e: e.memset(nkmax[:], 0.0), tok=True)

        def load_w(dst, src_dram, wait=()):
            K, N = src_dram.shape
            nchunk = K // 128
            ap = src_dram.ap().rearrange("(c p) n -> p c n", p=128)
            dstap = dst[:, 0:nchunk * N].rearrange("p (c n) -> p c n", c=nchunk)
            return S_.do("pool", lambda e: e.dma_start(out=dstap, in_=ap),
                         wait=wait, tok=True, dma=True)

        t_wk = load_w(wA, wk_d)
        t_wv = load_w(wB, wv_d)

        pn16_guard = [None]
        pn1_guard = [None]
        nm16_guard = [None]
        pb_guard = [None]
        pr_guard = [None]
        tr_guard = [None]

        bias_toks = {None: None}
        if rope:
            t_g = S_.do("pool", lambda e: e.dma_start(
                out=gcol[:], in_=lng_d.ap().rearrange("o (c p) -> p (o c)", p=128),
                allow_slow_non_contiguous=True), tok=True, dma=True)
            t_be = S_.do("pool", lambda e: e.dma_start(
                out=betac[:], in_=lnb_d.ap().rearrange("o (c p) -> p (o c)", p=128),
                allow_slow_non_contiguous=True), tok=True, dma=True)
            t_bo = S_.do("pool", lambda e: e.dma_start(
                out=brows[0:1, 3 * 1024:4 * 1024], in_=bout_d.ap()), tok=True, dma=True)

            def emit_bterm(wbuf, w_tok, col_off):
                last = None
                for dg in range(2):
                    mt = None
                    for ec in range(8):
                        w = [w_tok, t_be, pb_guard[0]] if ec == 0 else []
                        mt = S_.do("pe", lambda e, dg=dg, ec=ec, wbuf=wbuf: e.matmul(
                            psum_b[0:1, dg * 512:(dg + 1) * 512], betac[:, ec:ec + 1],
                            wbuf[:, ec * 1024 + dg * 512: ec * 1024 + (dg + 1) * 512],
                            start=(ec == 0), stop=(ec == 7)), wait=w, tok=(ec == 7))
                    last = S_.do("dve", lambda e, dg=dg, col_off=col_off: e.tensor_copy(
                        brows[0:1, col_off + dg * 512: col_off + (dg + 1) * 512],
                        psum_b[0:1, dg * 512:(dg + 1) * 512]), wait=[mt], tok=True)
                pb_guard[0] = last
                return last

            def emit_gscale(wbuf, w_tok, prev_need=()):
                tk = None
                for c in range(8):
                    tk = S_.do("act", lambda e, c=c, wbuf=wbuf: e.activation(
                        wbuf[:, c * 1024:(c + 1) * 1024], wbuf[:, c * 1024:(c + 1) * 1024],
                        AF.Copy, scale=gcol[:, c:c + 1]),
                        wait=([w_tok, t_g] + list(prev_need)) if c == 0 else [], tok=(c == 7))
                return tk

            t_bk = emit_bterm(wA, t_wk, 1 * 1024)
            t_bv = emit_bterm(wB, t_wv, 2 * 1024)
            t_wk = emit_gscale(wA, t_wk, prev_need=[t_bk])
            t_wv = emit_gscale(wB, t_wv, prev_need=[t_bv])
            bias_toks = {0: None, 1024: t_bk, 2048: t_bv}

        def emit_ln(xtb, stat, dst, wait, cache_to=None, s0=0):
            c = lambda k: stat[:, s0 + k:s0 + k + 1]
            t1 = S_.do("dve", lambda e: e.reduce_sum(c(0), xtb[:], axis=AX.X),
                       wait=wait, tok=True)
            t2 = S_.do("act", lambda e: e.activation(sqln[:], xtb[:], AF.Square,
                                                     accum_out=c(1)), wait=wait, tok=True)

            S_.do("dve", lambda e: e.tensor_scalar(c(2), c(0), 1.0 / D, None, ALU.mult),
                  wait=[t1])
            S_.do("dve", lambda e: e.tensor_scalar(c(3), c(1), 1.0 / D, None, ALU.mult),
                  wait=[t2])
            S_.do("dve", lambda e: e.tensor_tensor(c(4), c(2), c(2), ALU.mult))
            S_.do("dve", lambda e: e.tensor_tensor(c(5), c(3), c(4), ALU.subtract))
            t3 = S_.do("dve", lambda e: e.tensor_scalar(c(5), c(5), EPS_LN, None, ALU.add),
                       tok=True)
            S_.do("act", lambda e: e.activation(c(6), c(5), AF.Ln), wait=[t3])
            t4 = S_.do("act", lambda e: e.activation(c(6), c(6), AF.Exp, scale=-0.5),
                       tok=True)
            t5 = S_.do("dve", lambda e: e.scalar_tensor_tensor(
                c(7), c(2), -1.0, c(6), ALU.mult, ALU.mult), wait=[t4], tok=True)
            if cache_to is not None:
                t5 = S_.do("dve", lambda e: e.tensor_copy(
                    statc[:, cache_to * 8:cache_to * 8 + 8], stat[:, s0:s0 + 8]),
                    wait=[t5], tok=True)
            t6 = S_.do("act", lambda e: e.activation(
                dst[:], xtb[:], AF.Identity, bias=c(7), scale=c(6)), wait=[t5], tok=True)
            return t6

        def emit_transpose_tile(src, wait, dst_emit):
            tts = None
            for c in range(8):
                w = (list(wait) + [tr_guard[0], t_id]) if c == 0 else []
                tts = S_.do("pe", lambda e, c=c, src=src: e.transpose(
                    psum_tr[:, c * 128:(c + 1) * 128], src[:, c * 128:(c + 1) * 128], idn[:]),
                    wait=w, tok=(c == 7))
            t_ev = dst_emit([tts])
            tr_guard[0] = t_ev
            return t_ev

        def emit_proj_tm(xT_tile, wbuf, w_tok, dst, wait, scale=1.0, bias_off=None):
            evs = []
            for dg in range(2):
                mts = None
                for ec in range(8):
                    w = (list(wait) + [w_tok, pr_guard[0]]) if ec == 0 else []
                    mts = S_.do("pe", lambda e, dg=dg, ec=ec, xT_tile=xT_tile, wbuf=wbuf: e.matmul(
                        psum_p[:, dg * 512:(dg + 1) * 512],
                        xT_tile[:, ec * 128:(ec + 1) * 128],
                        wbuf[:, ec * 1024 + dg * 512: ec * 1024 + (dg + 1) * 512],
                        start=(ec == 0), stop=(ec == 7 and bias_off is None)),
                        wait=w, tok=(ec == 7 and bias_off is None))
                if bias_off is not None:
                    mts = S_.do("pe", lambda e, dg=dg, bias_off=bias_off: e.matmul(
                        psum_p[:, dg * 512:(dg + 1) * 512], onesr[:, :],
                        brows[0:1, bias_off + dg * 512: bias_off + (dg + 1) * 512],
                        start=False, stop=True),
                        wait=[t_or, bias_toks.get(bias_off)], tok=True)
                evs.append(S_.do("act", lambda e, dg=dg, dst=dst: e.activation(
                    dst[:, dg * 512:(dg + 1) * 512], psum_p[:, dg * 512:(dg + 1) * 512],
                    AF.Copy, scale=scale), wait=[mts], tok=True))
            t_done = evs[-1]
            pr_guard[0] = t_done
            return t_done

        if rope:
            def emit_rope(tile, ti, wait, cosd, sind):
                t_c = S_.do("sp", lambda e, ti=ti: e.dma_start(
                    out=ropes[:, 0:512], in_=cosd.ap()[ti * 128:(ti + 1) * 128, :]),
                    wait=wait, tok=True, dma=True)
                t_s = S_.do("sp", lambda e, ti=ti: e.dma_start(
                    out=ropes[:, 512:1024], in_=sind.ap()[ti * 128:(ti + 1) * 128, :]),
                    wait=wait, tok=True, dma=True)
                r3 = lambda ap: ap.rearrange("p (h k) -> p h k", k=32)
                x1 = tile[:, :].rearrange("p (h k) -> p h k", k=64)[:, :, 0:32]
                x2 = tile[:, :].rearrange("p (h k) -> p h k", k=64)[:, :, 32:64]
                cos = r3(ropes[:, 0:512])
                sin = r3(ropes[:, 512:1024])
                ta = S_.do("dve", lambda e: e.tensor_tensor(
                    r3(ropes[:, 1024:1536]), x1, cos, ALU.mult),
                    wait=[t_c] + list(wait), tok=True)
                S_.do("dve", lambda e: e.tensor_tensor(
                    r3(ropes[:, 1536:2048]), x2, sin, ALU.mult), wait=[t_s])
                S_.do("dve", lambda e: e.tensor_tensor(r3(rsc[0][:, :]), x2, cos, ALU.mult))
                S_.do("dve", lambda e: e.tensor_tensor(r3(rsc[1][:, :]), x1, sin, ALU.mult))
                S_.do("dve", lambda e: e.tensor_tensor(
                    x1, r3(ropes[:, 1024:1536]), r3(ropes[:, 1536:2048]), ALU.subtract))
                tf = S_.do("dve", lambda e: e.tensor_tensor(
                    x2, r3(rsc[0][:, :]), r3(rsc[1][:, :]), ALU.add), tok=True)
                return tf

        def emit_norm_update(kst, wait):
            t_sq = S_.do("dve", lambda e, kst=kst: e.tensor_tensor(
                sqkv[:], kst[:], kst[:], ALU.mult), wait=wait, tok=True)
            t_n2 = S_.do("dve", lambda e: e.tensor_reduce(
                n2t[:], sqkv[:].rearrange("p (h k) -> p h k", k=64), AX.X, ALU.add),
                wait=[t_sq], tok=True)
            t_n2 = S_.do("dve", lambda e: e.tensor_copy(n2tb[:], n2t[:]),
                         wait=[t_n2], tok=True)
            t_t1 = S_.do("pe", lambda e: e.transpose(psum_h16[:, :], n2tb[:], idn[:]),
                         wait=[t_n2, pn16_guard[0], t_id], tok=True)
            t_r = S_.do("dve", lambda e: e.reduce_max(nm16[:, 0:1], psum_h16[:, :], axis=AX.X),
                        wait=[t_t1, nm16_guard[0]], tok=True)
            pn16_guard[0] = t_r
            t_t2 = S_.do("pe", lambda e: e.transpose(psum_h1[0:1, 0:16], nm16[:, 0:1], idn16[:]),
                         wait=[t_r, pn1_guard[0], t_id16], tok=True)
            nm16_guard[0] = t_t2
            t_mx = S_.do("dve", lambda e: e.tensor_tensor(
                nkmax[:], nkmax[:], psum_h1[0:1, 0:16], ALU.max), wait=[t_t2, t_nk0], tok=True)
            pn1_guard[0] = t_mx
            return t_mx, [t_sq, t_n2]

        kT_view = kT_d.ap().rearrange("(c f) t -> f c t", f=128)
        vT_view = vT_d.ap().rearrange("(c f) t -> f c t", f=128)
        xh_guard = [None, None]
        kvK_guard = [None]
        kvV_guard = [None]
        tst_guard = [None, None]
        nk_last = [t_nk0]

        for i in range(NBT):
            p = i % 2
            xtb, stat, xh, xT = xt[p], stats[p], xhat[p], xhTt[p]
            t_ld = S_.do("sp", lambda e, i=i, xtb=xtb: e.dma_start(
                out=xtb[:], in_=x_b.ap()[i * 128:(i + 1) * 128, :]),
                wait=[xh_guard[p]], tok=True, dma=True)
            t_ln = emit_ln(xtb, stat, xh, [t_ld])
            if rope:
                t_ln = emit_ln(xh, stat, xh, [t_ln], s0=8)
            t_xT = emit_transpose_tile(xh, [t_ln], lambda wt, xT=xT: S_.do(
                "dve", lambda e: e.tensor_copy(xT[:], psum_tr[:]), wait=wt, tok=True))
            xh_guard[p] = t_xT

            t_k = emit_proj_tm(xT, wA, t_wk, kvK, [t_xT, kvK_guard[0]], scale=0.125,
                               bias_off=(1024 if rope else None))
            if rope:
                t_k = emit_rope(kvK, i, [t_k], cosb_d, sinb_d)
            t_mx, sq_toks = emit_norm_update(kvK, [t_k])
            nk_last[0] = t_mx
            t_kT = emit_transpose_tile(kvK, [t_k] + sq_toks,
                                       lambda wt: S_.do("dve", lambda e: e.tensor_copy(
                                           tstg[0][:], psum_tr[:]),
                                           wait=wt + [tst_guard[0]], tok=True))
            t_kst = S_.do("sp", lambda e, i=i: e.dma_start(
                out=kT_view[:, :, i * 128:(i + 1) * 128],
                in_=tstg[0][:, :].rearrange("p (c t) -> p c t", c=8)),
                wait=[t_kT], tok=True, dma=True)
            tst_guard[0] = t_kst
            kvK_guard[0] = t_kT

            t_v = emit_proj_tm(xT, wB, t_wv, kvV, [t_xT, kvV_guard[0]],
                               bias_off=(2048 if rope else None))
            t_vT = emit_transpose_tile(kvV, [t_v],
                                       lambda wt: S_.do("dve", lambda e: e.tensor_copy(
                                           tstg[1][:], psum_tr[:]),
                                           wait=wt + [tst_guard[1]], tok=True))
            t_vst = S_.do("sp", lambda e, i=i: e.dma_start(
                out=vT_view[:, :, i * 128:(i + 1) * 128],
                in_=tstg[1][:, :].rearrange("p (c t) -> p c t", c=8)),
                wait=[t_vT], tok=True, dma=True)
            tst_guard[1] = t_vst
            kvV_guard[0] = t_vT

        kv_done = [tst_guard[0], tst_guard[1]]

        t_wq = load_w(wA, wq_d, wait=[kvK_guard[0]])
        if rope:
            t_bq = emit_bterm(wA, t_wq, 0)
            t_wq = emit_gscale(wA, t_wq, prev_need=[t_bq])
            bias_toks[0] = t_bq
        t_wp = load_w(wB, wp_d, wait=[kvV_guard[0]])

        t_nkb = S_.do("dve", lambda e: e.tensor_copy(nkb[:], nkmax[:]), wait=[nk_last[0]], tok=True)
        t_mkbm = S_.do("pe", lambda e: e.matmul(psum_p[:, 0:16], onesr[:, :], nkb[0:1, :],
                                                start=True, stop=True),
                       wait=[t_nkb, t_or, pr_guard[0]], tok=True)
        t_mkb = S_.do("dve", lambda e: e.tensor_copy(mkb[:], psum_p[:, 0:16]),
                      wait=[t_mkbm], tok=True)
        pr_guard[0] = t_mkb

        for i in range(NMT):
            p = i % 2
            xtb, stat, xh, xT = xt[p], stats[p], xhat[p], xhTt[p]
            t_ld = S_.do("sp", lambda e, i=i, xtb=xtb: e.dma_start(
                out=xtb[:], in_=x_my.ap()[i * 128:(i + 1) * 128, :]),
                wait=[xh_guard[p]], tok=True, dma=True)
            t_ln = emit_ln(xtb, stat, xh, [t_ld], cache_to=i)
            if rope:
                t_ln = emit_ln(xh, stat, xh, [t_ln], s0=8)
            t_xT = emit_transpose_tile(xh, [t_ln], lambda wt, xT=xT: S_.do(
                "dve", lambda e: e.tensor_copy(xT[:], psum_tr[:]), wait=wt, tok=True))
            xh_guard[p] = t_xT

            t_q = emit_proj_tm(xT, wA, t_wq, kvK, [t_xT, kvK_guard[0]],
                               bias_off=(0 if rope else None))
            if rope:
                t_q = emit_rope(kvK, i, [t_q], cosr_d, sinr_d)
            t_sq = S_.do("dve", lambda e: e.tensor_tensor(sqkv[:], kvK[:], kvK[:], ALU.mult),
                         wait=[t_q], tok=True)
            t_n2 = S_.do("dve", lambda e: e.tensor_reduce(
                n2t[:], sqkv[:].rearrange("p (h k) -> p h k", k=64), AX.X, ALU.add),
                wait=[t_sq], tok=True)
            t_c1 = S_.do("dve", lambda e: e.tensor_tensor(ctile[:], n2t[:], mkb[:], ALU.mult),
                         wait=[t_n2, t_mkb], tok=True)
            t_c2 = S_.do("act", lambda e: e.activation(ctile[:], ctile[:], AF.Sqrt, scale=1.1025),
                         wait=[t_c1], tok=True)
            for h in range(NH):
                t_tc = S_.do("pe", lambda e, h=h: e.transpose(
                    psum_h1[0:1, :], ctile[:, h:h + 1], idn[:]),
                    wait=[t_c2, pn1_guard[0], t_id], tok=True)
                t_wr = S_.do("dve", lambda e, h=h, i=i: e.tensor_scalar(
                    qtil[h][64:65, i * 128:(i + 1) * 128], psum_h1[0:1, :], -1.0, None,
                    ALU.mult), wait=[t_tc], tok=True)
                pn1_guard[0] = t_wr

            def dst_q(wt, i=i):
                last = None
                for c in range(8):
                    for hh in range(2):
                        last = S_.do("dve", lambda e, c=c, hh=hh, i=i: e.tensor_copy(
                            qtil[2 * c + hh][0:64, i * 128:(i + 1) * 128],
                            psum_tr[hh * 64:(hh + 1) * 64, c * 128:(c + 1) * 128]),
                            wait=(wt if (c == 0 and hh == 0) else []),
                            tok=(c == 7 and hh == 1))
                return last
            t_qT = emit_transpose_tile(kvK, [t_q, t_sq], dst_q)
            kvK_guard[0] = t_qT

        q_done = kvK_guard[0]

        p1.close()
        ps1.close()
        fl = ExitStack()
        psum_s = [fl.enter_context(nc.psum_tensor(f"psum_s{i}", [128, 1024], F32))
                  for i in range(2)]
        psum_av = fl.enter_context(nc.psum_tensor("psum_av", [65, 512], F32))
        psum_bc = fl.enter_context(nc.psum_tensor("psum_bc", [64, 512], F32))

        ktil = sb(fl, "ktil", [65, 8192], BF16)
        vt = sb(fl, "vt", [64, 8192], BF16)
        vtil = sb(fl, "vtil", [128, 65 * 64], BF16)
        pt = sb(fl, "pt", [128, 2 * 1024], BF16)
        rp = sb(fl, "rp", [1, 512], F32)
        bb = sb(fl, "bb", [64, 512], F32)

        t_k1 = S_.do("dve", lambda e: e.memset(ktil[64:65, :], 1.0),
                     wait=[q_done], tok=True)
        t_v1 = S_.do("dve", lambda e: e.memset(
            vtil[:, :].rearrange("p (k c) -> p k c", c=65)[:, :, 64:65], 1.0), tok=True)

        NKT = S // 128
        NG = NKT // 2
        k_guard = [None]
        vtr_guard = [None]
        av_guard = [None]
        bc_guard = [None]
        s_guard = [None, None]
        pt_guard = [None, None]
        aT_last = [None]

        for h in range(NH):
            t_kld = S_.do("sp", lambda e, h=h: e.dma_start(
                out=ktil[0:64, :], in_=kT_d.ap()[h * 64:(h + 1) * 64, :]),
                wait=[kv_done[0], k_guard[0], t_k1, q_done], tok=True, dma=True)
            t_vld = S_.do("sp", lambda e, h=h: e.dma_start(
                out=vt[0:64, :], in_=vT_d.ap()[h * 64:(h + 1) * 64, :]),
                wait=[kv_done[1], k_guard[0], t_k1, q_done], tok=True, dma=True)
            t_vtr = None
            for r in range(4):
                tts = None
                for j in range(16):
                    kt = r * 16 + j
                    w = [t_vld, tr_guard[0], t_id64, t_v1, q_done] if j == 0 else []
                    tts = S_.do("pe", lambda e, kt=kt, j=j: e.transpose(
                        psum_tr[:, j * 64:(j + 1) * 64], vt[0:64, kt * 128:(kt + 1) * 128],
                        idn64[:]), wait=w, tok=(j == 15))
                t_vtr = S_.do("dve", lambda e, r=r: e.tensor_copy(
                    vtil[:, :].rearrange("p (k c) -> p k c", c=65)[:, r * 16:(r + 1) * 16, 0:64],
                    psum_tr[:, :].rearrange("p (j c) -> p j c", c=64)),
                    wait=[tts, vtr_guard[0]], tok=True)
                tr_guard[0] = t_vtr
            vtr_guard[0] = t_vtr

            for qg in range(4):
                last_av = None
                for g in range(NG):
                    par = g % 2
                    mts = None
                    for j in range(2):
                        kt = g * 2 + j
                        w = [t_kld, t_vtr, q_done, s_guard[par]] if j == 0 else []
                        mts = S_.do("pe", lambda e, kt=kt, j=j, par=par, h=h, qg=qg: e.matmul(
                            psum_s[par][:, j * 512:(j + 1) * 512],
                            ktil[:, kt * 128:(kt + 1) * 128],
                            qtil[h][:, qg * 512:(qg + 1) * 512],
                            start=True, stop=True), wait=w, tok=(j == 1))
                    t_e = S_.do("act", lambda e, par=par: e.activation(
                        pt[:, par * 1024:(par + 1) * 1024], psum_s[par][:, :], AF.Exp),
                        wait=[mts, pt_guard[par]], tok=True)
                    s_guard[par] = t_e
                    av = None
                    for j in range(2):
                        kt = g * 2 + j
                        w = [t_e]
                        if g == 0 and j == 0:
                            w.append(av_guard[0])
                        av = S_.do("pe", lambda e, kt=kt, j=j, par=par: e.matmul(
                            psum_av[:, :], vtil[:, kt * 65:kt * 65 + 65],
                            pt[:, par * 1024 + j * 512: par * 1024 + (j + 1) * 512],
                            start=(kt == 0), stop=(kt == NKT - 1)),
                            wait=w, tok=(j == 1))
                    pt_guard[par] = av
                    last_av = av
                t_rp = S_.do("dve", lambda e: e.reciprocal(rp[0:1, :], psum_av[64:65, :]),
                             wait=[last_av], tok=True)
                t_bc = S_.do("pe", lambda e: e.matmul(
                    psum_bc[:, :], onesrf[0:1, 0:64], rp[0:1, :], start=True, stop=True),
                    wait=[t_rp, t_orf, bc_guard[0]], tok=True)
                t_bb = S_.do("dve", lambda e: e.tensor_copy(bb[:], psum_bc[:, :]),
                             wait=[t_bc], tok=True)
                bc_guard[0] = t_bb
                t_aT = S_.do("dve", lambda e, h=h, qg=qg: e.tensor_tensor(
                    aT[(h % 2) * 64:(h % 2) * 64 + 64,
                       (h // 2) * 2048 + qg * 512:(h // 2) * 2048 + (qg + 1) * 512],
                    psum_av[0:64, :], bb[:], ALU.mult), wait=[t_bb], tok=True)
                av_guard[0] = t_aT
                aT_last[0] = t_aT
            k_guard[0] = av_guard[0]

        if _DEBUG:
            S_.do("sp", lambda e: e.dma_start(out=dbg_k.ap(), in_=kT_d.ap()),
                  wait=[kv_done[0]], tok=True, dma=True)
            S_.do("sp", lambda e: e.dma_start(out=dbg_v.ap(), in_=vT_d.ap()),
                  wait=[kv_done[1]], tok=True, dma=True)
            S_.do("sp", lambda e: e.dma_start(out=dbg_nk.ap(), in_=nkmax[:]),
                  wait=[nk_last[0]], tok=True, dma=True)
            S_.do("sp", lambda e: e.dma_start(out=dbg_q0.ap(), in_=qtil[0][:]),
                  wait=[q_done], tok=True, dma=True)
            S_.do("sp", lambda e: e.dma_start(out=dbg_q7.ap(), in_=qtil[7][:]),
                  wait=[q_done], tok=True, dma=True)
            S_.do("sp", lambda e: e.dma_start(out=dbg_aT.ap(), in_=aT[:]),
                  wait=[aT_last[0]], tok=True, dma=True)

        fl.close()
        q_stack.close()

        e_stack = ExitStack()
        if variant == "l0a":
            a2T = sb(e_stack, "a2T", [128, 8 * 2048], BF16)
            t_gl = None
            for c in range(8):
                t_gl = S_.do("act", lambda e, c=c: e.activation(
                    a2T[:, c * 2048:(c + 1) * 2048], aT[:, c * 2048:(c + 1) * 2048], AF.Gelu),
                    wait=([aT_last[0]] if c == 0 else []), tok=(c == 7))
        else:
            a2T = aT
            t_gl = aT_last[0]

        psum_o = [e_stack.enter_context(nc.psum_tensor(f"psum_o{i}", [128, 512], F32))
                  for i in range(2)]
        xhstage = sb(e_stack, "xhstage", [128, 1024], BF16)
        xores = [sb(e_stack, f"xores{i}", [128, 1024], F32) for i in range(2)]
        xo_guard = [None, None]
        o_guard = [None, None]
        xhs_guard = [None]
        last_store = [None]
        for i in range(NMT):
            p = i % 2
            xtb = xt[p]
            t_ld = S_.do("sp", lambda e, i=i, xtb=xtb: e.dma_start(
                out=xtb[:], in_=x_my.ap()[i * 128:(i + 1) * 128, :]),
                wait=[xh_guard[p]], tok=True, dma=True)
            t_xh = None
            if variant == "l0a":
                t_xh = S_.do("act", lambda e, i=i, xtb=xtb: e.activation(
                    xhstage[:], xtb[:], AF.Identity, bias=statc[:, i * 8 + 7:i * 8 + 8],
                    scale=statc[:, i * 8 + 6:i * 8 + 7]),
                    wait=[t_ld, t_gl, xhs_guard[0]], tok=True)
            evs = []
            for dg in range(2):
                mts = None
                for c in range(8):
                    w = [t_wp, t_gl, o_guard[dg]] if c == 0 else []
                    mts = S_.do("pe", lambda e, i=i, dg=dg, c=c: e.matmul(
                        psum_o[dg][:, :],
                        a2T[:, c * 2048 + i * 128: c * 2048 + (i + 1) * 128],
                        wB[:, c * 1024 + dg * 512: c * 1024 + (dg + 1) * 512],
                        start=(c == 0), stop=(c == 7 and not rope)),
                        wait=w, tok=(c == 7 and not rope))
                if rope:
                    mts = S_.do("pe", lambda e, dg=dg: e.matmul(
                        psum_o[dg][:, :], onesr[:, :],
                        brows[0:1, 3 * 1024 + dg * 512: 3 * 1024 + (dg + 1) * 512],
                        start=False, stop=True), wait=[t_bo, t_or], tok=True)
                if variant == "l0a":
                    t_e1 = S_.do("dve", lambda e, dg=dg, p=p: e.tensor_tensor(
                        xores[p][:, dg * 512:(dg + 1) * 512], psum_o[dg][:, :],
                        xhstage[:, dg * 512:(dg + 1) * 512], ALU.add),
                        wait=[mts, t_xh, xo_guard[p]], tok=True)
                    t_e2 = S_.do("dve", lambda e, dg=dg, p=p, xtb=xtb: e.tensor_tensor(
                        xores[p][:, dg * 512:(dg + 1) * 512],
                        xores[p][:, dg * 512:(dg + 1) * 512],
                        xtb[:, dg * 512:(dg + 1) * 512], ALU.add),
                        wait=[t_ld], tok=True)
                else:
                    t_e2 = S_.do("dve", lambda e, dg=dg, p=p, xtb=xtb: e.tensor_tensor(
                        xores[p][:, dg * 512:(dg + 1) * 512], psum_o[dg][:, :],
                        xtb[:, dg * 512:(dg + 1) * 512], ALU.add),
                        wait=[mts, t_ld, xo_guard[p]], tok=True)
                o_guard[dg] = t_e2
                evs.append(t_e2)
            if variant == "l0a":
                xhs_guard[0] = evs[-1]
            t_st = S_.do("sp", lambda e, i=i, p=p: e.dma_start(
                out=xo.ap()[i * 128:(i + 1) * 128, :], in_=xores[p][:]),
                wait=evs, tok=True, dma=True)
            xo_guard[p] = t_st
            xh_guard[p] = t_st
            last_store[0] = t_st

        e_stack.close()
        S_.emit(nc, final_waits=[last_store[0]])

    return nc



_BUILD_CACHE = {}


def _get(name, fn):
    if name not in _BUILD_CACHE:
        _BUILD_CACHE[name] = fn()
    return _BUILD_CACHE[name]


def _rope_tables_halfsplit(n):
    half = 32
    inv = 1.0 / (10000.0 ** (np.arange(half, dtype=np.float32) / half))
    freqs = np.arange(n, dtype=np.float32)[:, None] * inv[None, :]
    cos = np.cos(freqs).astype(np.float32)
    sin = np.sin(freqs).astype(np.float32)
    return (np.ascontiguousarray(np.tile(cos, (1, NH))),
            np.ascontiguousarray(np.tile(sin, (1, NH))))


def run_l0a(x, Wq, Wk, Wv, Wp):
    nc = _get("l0a", lambda: build_attn_l0("l0a"))
    in_maps = []
    for c in range(N_CORES):
        b = c // 4
        in_maps.append({
            "x_my": x[c * T:(c + 1) * T],
            "x_b": x[b * S:(b + 1) * S],
            "wq": Wq, "wk": Wk, "wv": Wv, "wp": Wp,
        })
    res = run_bass_kernel_spmd(nc, in_maps, core_ids=list(range(N_CORES)))
    return np.concatenate([res.results[c]["xo"] for c in range(N_CORES)], axis=0)


def run_l0b(x, w_qkv, w_out, ln_g, ln_b, b_out):
    nc = _get("l0b", lambda: build_attn_l0("l0b"))
    wqkvT = np.ascontiguousarray(w_qkv.T)
    woutT = np.ascontiguousarray(w_out.T)
    cosb, sinb = _rope_tables_halfsplit(S)
    in_maps = []
    for c in range(N_CORES):
        b = c // 4
        off = (c % 4) * T
        in_maps.append({
            "x_my": x[c * T:(c + 1) * T],
            "x_b": x[b * S:(b + 1) * S],
            "wq": np.ascontiguousarray(wqkvT[:, 0:D]),
            "wk": np.ascontiguousarray(wqkvT[:, D:2 * D]),
            "wv": np.ascontiguousarray(wqkvT[:, 2 * D:3 * D]),
            "wp": woutT,
            "lng": np.ascontiguousarray(ln_g.reshape(1, D)),
            "lnb": np.ascontiguousarray(ln_b.reshape(1, D)),
            "bout": np.ascontiguousarray(b_out.reshape(1, D)),
            "cosr": np.ascontiguousarray(cosb[off:off + T]),
            "sinr": np.ascontiguousarray(sinb[off:off + T]),
            "cosb": cosb, "sinb": sinb,
        })
    res = run_bass_kernel_spmd(nc, in_maps, core_ids=list(range(N_CORES)))
    return np.concatenate([res.results[c]["xo"] for c in range(N_CORES)], axis=0)


def kernel(**inputs):
    ii = {k: np.ascontiguousarray(np.asarray(v, dtype=np.float32))
          for k, v in inputs.items()}
    x = ii["L0_attn_a_X"].reshape(B * S, D)
    x = run_l0a(x, ii["L0_attn_a_Wq"], ii["L0_attn_a_Wk"],
                ii["L0_attn_a_Wv"], ii["L0_attn_a_Wpost"])
    x = run_l0ffn(x, ii["L0_ffn_Wg"], ii["L0_ffn_w1"], ii["L0_ffn_w2"])
    x = run_l0b(x, ii["L0_attn_b_w_qkv"], ii["L0_attn_b_w_out"],
                ii["L0_attn_b_ln_g"], ii["L0_attn_b_ln_b"], ii["L0_attn_b_b_out"])
    x = run_l0ffn(x, ii["L0_ffn_Wg"], ii["L0_ffn_w1"], ii["L0_ffn_w2"])
    parts = run_l1a(x, ii["L1_attn_a_Wq"], ii["L1_attn_a_Wk"],
                    ii["L1_attn_a_Wv"], ii["L1_attn_a_Wo"])
    x = run_l1ffn(x, ii["L1_ffn_router_w"], ii["L1_ffn_wg_r"], ii["L1_ffn_wu_r"],
                  ii["L1_ffn_wd_r"], ii["L1_ffn_wg_s"], ii["L1_ffn_wu_s"],
                  ii["L1_ffn_wd_s"], partials=parts)
    x = run_l1b(x, ii["L1_attn_b_wq"], ii["L1_attn_b_wk"], ii["L1_attn_b_wv"],
                ii["L1_attn_b_wgq"], ii["L1_attn_b_wgk"], ii["L1_attn_b_wgv"],
                ii["L1_attn_b_r_out_w"], ii["L1_attn_b_gate_local_w"],
                ii["L1_attn_b_gate_remote_w"], ii["L1_attn_b_w_out"])
    x = run_l1ffn(x, ii["L1_ffn_router_w"], ii["L1_ffn_wg_r"], ii["L1_ffn_wu_r"],
                  ii["L1_ffn_wd_r"], ii["L1_ffn_wg_s"], ii["L1_ffn_wu_s"],
                  ii["L1_ffn_wd_s"])
    return x.reshape(B, S, D)



NE = 16
CAP = 1024
ER = 1152
NFT = (B * S) // 128
FF = 4096


def build_ffn_a():
    nc = new_nc("ffna")
    x_all = nc.dram_tensor("x_all", [B * S, D], F32, kind="ExternalInput")
    wg_d = nc.dram_tensor("wg", [D, NE], F32, kind="ExternalInput")
    w1_d = nc.dram_tensor("w1", [2 * D, FF], F32, kind="ExternalInput")
    w2_d = nc.dram_tensor("w2", [2 * FF, D], F32, kind="ExternalInput")
    ebase_d = nc.dram_tensor("ebase", [128, 1], F32, kind="ExternalInput")
    eo = nc.dram_tensor("eo", [2 * ER, D], F32, kind="ExternalOutput")
    rmeta = nc.dram_tensor("rmeta", [B * S, 4], F32, kind="ExternalOutput")
    if _DEBUG:
        dbg_lg = nc.dram_tensor("dbg_lg", [B * S, NE], F32, kind="ExternalOutput")

    xh_d = nc.dram_tensor("xh_d", [B * S, D], BF16)
    oh_d = nc.dram_tensor("oh_d", [B * S, 20], F32)
    base_d = nc.dram_tensor("base_d", [128, NE], F32)
    disp_d = nc.dram_tensor("disp_d", [NE * ER, D], BF16)

    i128c = nc.inline_tensor(np.eye(128, dtype=BF), name="ident128")
    i128fc = nc.inline_tensor(np.eye(128, dtype=np.float32), name="ident128f")
    triuc = nc.inline_tensor(np.triu(np.ones((128, 128), np.float32), 1), name="triu_c")
    onesfc = nc.inline_tensor(np.ones((128, 1), dtype=np.float32), name="onescol")
    ones1fc = nc.inline_tensor(np.ones((1, 128), dtype=np.float32), name="onesrowf")
    eiotac = nc.inline_tensor(
        np.tile(np.arange(NE, dtype=np.float32)[None, :], (128, 1)), name="eiota_c")
    i16fc = nc.inline_tensor(np.eye(16, dtype=np.float32), name="ident16f")

    S_ = Sched()
    st = ExitStack()
    with st:
        def sb(stk, name, shape, dt):
            return stk.enter_context(nc.sbuf_tensor(name, shape, dt))

        idn = sb(st, "idn", [128, 128], BF16)
        idnf = sb(st, "idnf", [128, 128], F32)
        triu = sb(st, "triu", [128, 128], F32)
        onesc = sb(st, "onesc", [128, 1], F32)
        onesrf = sb(st, "onesrf", [1, 128], F32)
        eiota = sb(st, "eiota", [128, NE], F32)
        idn16f = sb(st, "idn16f", [16, 16], F32)
        wg = sb(st, "wg_s", [128, 8 * NE], BF16)
        ebase = sb(st, "ebase_s", [128, 1], F32)
        xt = [sb(st, f"xt{i}", [128, 1024], F32) for i in range(2)]
        stats = [sb(st, f"stats{i}", [128, 16], F32) for i in range(2)]
        sqln = sb(st, "sqln", [128, 1024], BF16)
        xhat = [sb(st, f"xhat{i}", [128, 1024], BF16) for i in range(2)]
        xhf = [sb(st, f"xhf{i}", [128, 1024], F32) for i in range(2)]
        xlo = [sb(st, f"xlo{i}", [128, 1024], BF16) for i in range(2)]
        xhTt = [sb(st, f"xhTt{i}", [128, 1024], BF16) for i in range(2)]
        xlTt = [sb(st, f"xlTt{i}", [128, 1024], BF16) for i in range(2)]
        wgf = sb(st, "wgf_s", [128, 8 * NE], F32)
        wglo = sb(st, "wglo_s", [128, 8 * NE], BF16)
        lgT = sb(st, "lgT", [16, 128], F32)
        lsb = sb(st, "lsb", [128, NE], F32)
        rstat = sb(st, "rstat", [128, 8], F32)
        ohg = [sb(st, f"ohg{i}", [128, 20], F32) for i in range(2)]
        scol = sb(st, "scol", [16, 128], F32)
        base_sb = sb(st, "base_sb", [128, NE], F32)
        rowi = sb(st, "rowi", [1, NE], F32)
        locw = sb(st, "locw", [128, NE], F32)
        meta = [sb(st, f"meta{i}", [128, 8], F32) for i in range(2)]
        slotI = [sb(st, f"slotI{i}", [128, 1], I32) for i in range(2)]
        zerot = sb(st, "zerot", [128, 1024], BF16)

        psum_tr = st.enter_context(nc.psum_tensor("psum_tr", [128, 1024], BF16))
        psum_r = st.enter_context(nc.psum_tensor("psum_r", [128, 512], F32))
        psum_c = st.enter_context(nc.psum_tensor("psum_c", [128, 16], F32))
        psum_s = st.enter_context(nc.psum_tensor("psum_cs", [16, 16], F32))
        psum_w = st.enter_context(nc.psum_tensor("psum_row", [1, 16], F32))

        t_id = S_.do("sp", lambda e: e.dma_start(out=idn[:], in_=i128c.ap()), tok=True, dma=True)
        t_idf = S_.do("sp", lambda e: e.dma_start(out=idnf[:], in_=i128fc.ap()), tok=True, dma=True)
        t_tri = S_.do("sp", lambda e: e.dma_start(out=triu[:], in_=triuc.ap()), tok=True, dma=True)
        t_oc = S_.do("sp", lambda e: e.dma_start(out=onesc[:], in_=onesfc.ap()), tok=True, dma=True)
        t_orf = S_.do("sp", lambda e: e.dma_start(out=onesrf[:], in_=ones1fc.ap()), tok=True, dma=True)
        t_ei = S_.do("sp", lambda e: e.dma_start(out=eiota[:], in_=eiotac.ap()), tok=True, dma=True)
        t_i16f = S_.do("sp", lambda e: e.dma_start(out=idn16f[:], in_=i16fc.ap()), tok=True, dma=True)
        t_eb = S_.do("sp", lambda e: e.dma_start(out=ebase[:], in_=ebase_d.ap()), tok=True, dma=True)
        t_wgf = S_.do("sp", lambda e: e.dma_start(
            out=wgf[:, :].rearrange("p (c n) -> p c n", c=8),
            in_=wg_d.ap().rearrange("(c p) n -> p c n", p=128)), tok=True, dma=True)
        t_wgh = S_.do("dve", lambda e: e.tensor_copy(wg[:, :], wgf[:, :]),
                      wait=[t_wgf], tok=True)
        t_wg = S_.do("dve", lambda e: e.tensor_tensor(wglo[:, :], wgf[:, :], wg[:, :],
                                                      ALU.subtract), tok=True)
        t_z = S_.do("dve", lambda e: e.memset(zerot[:], 0.0), tok=True)

        def emit_ln(xtb, stat, dst, wait):
            c = lambda k: stat[:, k:k + 1]
            t1 = S_.do("dve", lambda e: e.reduce_sum(c(0), xtb[:], axis=AX.X),
                       wait=wait, tok=True)
            t2 = S_.do("act", lambda e: e.activation(sqln[:], xtb[:], AF.Square,
                                                     accum_out=c(1)), wait=wait, tok=True)
            S_.do("dve", lambda e: e.tensor_scalar(c(2), c(0), 1.0 / D, None, ALU.mult),
                  wait=[t1])
            S_.do("dve", lambda e: e.tensor_scalar(c(3), c(1), 1.0 / D, None, ALU.mult),
                  wait=[t2])
            S_.do("dve", lambda e: e.tensor_tensor(c(4), c(2), c(2), ALU.mult))
            S_.do("dve", lambda e: e.tensor_tensor(c(5), c(3), c(4), ALU.subtract))
            t3 = S_.do("dve", lambda e: e.tensor_scalar(c(5), c(5), EPS_LN, None, ALU.add),
                       tok=True)
            S_.do("act", lambda e: e.activation(c(6), c(5), AF.Ln), wait=[t3])
            t4 = S_.do("act", lambda e: e.activation(c(6), c(6), AF.Exp, scale=-0.5),
                       tok=True)
            t5 = S_.do("dve", lambda e: e.scalar_tensor_tensor(
                c(7), c(2), -1.0, c(6), ALU.mult, ALU.mult), wait=[t4], tok=True)
            t6 = S_.do("act", lambda e: e.activation(
                dst[:], xtb[:], AF.Identity, bias=c(7), scale=c(6)), wait=[t5], tok=True)
            return t6

        tr_guard = [None]

        def emit_transpose_tile(src, wait, dst_emit, fp32=False):
            tts = None
            for c in range(8):
                w = (list(wait) + [tr_guard[0], t_id]) if c == 0 else []
                tts = S_.do("pe", lambda e, c=c, src=src: e.transpose(
                    psum_tr[:, c * 128:(c + 1) * 128], src[:, c * 128:(c + 1) * 128], idn[:]),
                    wait=w, tok=(c == 7))
            t_ev = dst_emit([tts])
            tr_guard[0] = t_ev
            return t_ev

        xh_guard = [None, None]
        pr_guard = [None]
        ps_guard = [None]
        xh_view = xh_d.ap()
        for i in range(NFT):
            p = i % 2
            xtb, stat, xh, xT, og = xt[p], stats[p], xhat[p], xhTt[p], ohg[p]
            xf, xl, xlT = xhf[p], xlo[p], xlTt[p]
            t_ld = S_.do("sp", lambda e, i=i, xtb=xtb: e.dma_start(
                out=xtb[:], in_=x_all.ap()[i * 128:(i + 1) * 128, :]),
                wait=[xh_guard[p]], tok=True, dma=True)
            t_ln = emit_ln(xtb, stat, xf, [t_ld])
            t_hi = S_.do("dve", lambda e, xh=xh, xf=xf: e.tensor_copy(xh[:], xf[:]),
                         wait=[t_ln], tok=True)
            t_lo = S_.do("dve", lambda e, xh=xh, xf=xf, xl=xl: e.tensor_tensor(
                xl[:], xf[:], xh[:], ALU.subtract), tok=True)
            t_xst = S_.do("sp", lambda e, i=i, xh=xh: e.dma_start(
                out=xh_view[i * 128:(i + 1) * 128, :], in_=xh[:]), wait=[t_hi],
                tok=True, dma=True)
            t_xT = emit_transpose_tile(xh, [t_hi], lambda wt, xT=xT: S_.do(
                "dve", lambda e: e.tensor_copy(xT[:], psum_tr[:]), wait=wt, tok=True))
            t_lT = emit_transpose_tile(xl, [t_lo], lambda wt, xlT=xlT: S_.do(
                "dve", lambda e: e.tensor_copy(xlT[:], psum_tr[:]), wait=wt, tok=True))
            mtoks = []
            for gi, (wop, xop) in enumerate(((wg, xT), (wglo, xT), (wg, xlT),
                                             (wglo, xlT))):
                mts = None
                for ec in range(8):
                    w = [t_xT, t_lT, t_wg, pr_guard[0]] if (gi == 0 and ec == 0) else []
                    mts = S_.do("pe", lambda e, gi=gi, ec=ec, wop=wop, xop=xop: e.matmul(
                        psum_r[0:16, gi * 128:(gi + 1) * 128],
                        wop[:, ec * NE:(ec + 1) * NE],
                        xop[:, ec * 128:(ec + 1) * 128],
                        start=(ec == 0), stop=(ec == 7)),
                        wait=w, tok=(ec == 7))
                mtoks.append(mts)
            t_c0 = S_.do("dve", lambda e: e.tensor_tensor(
                lgT[:, :], psum_r[0:16, 0:128], psum_r[0:16, 128:256], ALU.add) if False
                else e.tensor_copy(lgT[:, :], psum_r[0:16, 0:128]), wait=mtoks, tok=True)
            t_c1 = S_.do("dve", lambda e: e.tensor_tensor(
                lgT[:, :], lgT[:, :], psum_r[0:16, 128:256], ALU.add), tok=True)
            S_.do("dve", lambda e: e.tensor_tensor(
                lgT[:, :], lgT[:, :], psum_r[0:16, 256:384], ALU.add))
            t_lgT = S_.do("dve", lambda e: e.tensor_tensor(
                lgT[:, :], lgT[:, :], psum_r[0:16, 384:512], ALU.add), tok=True)
            t_ltr = S_.do("pe", lambda e: e.transpose(psum_r[:, 128:144], lgT[:, :],
                                                      idn16f[:]),
                          wait=[t_lgT, t_i16f], tok=True)
            t_l = S_.do("dve", lambda e: e.tensor_copy(lsb[:], psum_r[:, 128:144]),
                        wait=[t_ltr], tok=True)
            pr_guard[0] = t_l
            if _DEBUG:
                S_.do("sp", lambda e, i=i: e.dma_start(
                    out=dbg_lg.ap()[i * 128:(i + 1) * 128, :], in_=lsb[:]),
                    wait=[t_l], tok=True, dma=True)
            t_m = S_.do("dve", lambda e: e.reduce_max(rstat[:, 0:1], lsb[:], axis=AX.X),
                        tok=True)
            t_neg = S_.do("dve", lambda e: e.tensor_scalar(rstat[:, 1:2], rstat[:, 0:1],
                                                           -1.0, None, ALU.mult), tok=True)
            t_ex = S_.do("act", lambda e: e.activation(
                locw[:, 0:16], lsb[:], AF.Exp,
                bias=rstat[:, 1:2], accum_out=rstat[:, 2:3]), wait=[t_neg], tok=True)
            t_g = S_.do("dve", lambda e: e.reciprocal(rstat[:, 3:4], rstat[:, 2:3]),
                        wait=[t_ex], tok=True)
            t_oh = S_.do("dve", lambda e, og=og: e.tensor_scalar(
                og[:, 0:16], lsb[:], rstat[:, 0:1], None, ALU.is_ge), tok=True)
            t_gt = S_.do("dve", lambda e, og=og: e.tensor_copy(og[:, 16:17], rstat[:, 3:4]),
                         tok=True)
            t_ost = S_.do("sp", lambda e, i=i, og=og: e.dma_start(
                out=oh_d.ap()[i * 128:(i + 1) * 128, :], in_=og[:]),
                wait=[t_oh, t_gt], tok=True, dma=True)
            t_cs = S_.do("pe", lambda e, og=og: e.matmul(
                psum_s[:, 0:1], og[:, 0:16], onesc[:], start=True, stop=True),
                wait=[t_oh, t_oc, ps_guard[0]], tok=True)
            t_sc = S_.do("dve", lambda e, i=i: e.tensor_copy(scol[:, i:i + 1],
                                                             psum_s[:, 0:1]),
                         wait=[t_cs], tok=True)
            ps_guard[0] = t_sc
            xh_guard[p] = t_ost

        t_scT = S_.do("pe", lambda e: e.transpose(psum_r[:, 0:16],
                                                  scol[:, :], idn16f[:]),
                      wait=[ps_guard[0], t_i16f, pr_guard[0]], tok=True)
        t_scT2 = S_.do("dve", lambda e: e.tensor_copy(base_sb[:], psum_r[:, 0:16]),
                       wait=[t_scT], tok=True)
        pr_guard[0] = t_scT2
        t_bm = S_.do("pe", lambda e: e.matmul(psum_c[:, :], triu[:, :], base_sb[:, :],
                                              start=True, stop=True),
                     wait=[t_scT2, t_tri], tok=True)
        t_bs = S_.do("dve", lambda e: e.tensor_copy(base_sb[:], psum_c[:, :]),
                     wait=[t_bm], tok=True)

        z_tok = None
        for r in range(2 * ER // 128):
            z_tok = S_.do("pool", lambda e, r=r: e.dma_start(
                out=disp_d.ap()[r * 128:(r + 1) * 128, :], in_=zerot[:]),
                wait=[t_z], tok=True, dma=True)

        sc_guard = [None, None]
        rm_guard = [None, None]
        pw_guard = [None]
        for i in range(NFT):
            p = i % 2
            og, xh, mt, sI = ohg[p], xhat[p], meta[p], slotI[p]
            t_lo = S_.do("sp", lambda e, i=i, og=og: e.dma_start(
                out=og[:], in_=oh_d.ap()[i * 128:(i + 1) * 128, :]),
                wait=[xh_guard[p], sc_guard[p]], tok=True, dma=True)
            t_lx = S_.do("sp", lambda e, i=i, xh=xh: e.dma_start(
                out=xh[:], in_=xh_d.ap()[i * 128:(i + 1) * 128, :]),
                wait=[xh_guard[p], sc_guard[p]], tok=True, dma=True)
            t_c1 = S_.do("pe", lambda e, og=og: e.matmul(
                psum_c[:, :], triu[:, :], og[:, 0:16], start=True, stop=False),
                wait=[t_lo, t_bs, pr_guard[0]], tok=True)
            t_c2 = S_.do("pe", lambda e, i=i: e.matmul(
                psum_w[:, :], idnf[:, i:i + 1], base_sb[:, :], start=True, stop=True),
                wait=[t_bs, pw_guard[0]], tok=True)
            t_rw = S_.do("dve", lambda e: e.tensor_copy(rowi[:], psum_w[:, :]),
                         wait=[t_c2], tok=True)
            pw_guard[0] = t_rw
            t_c3 = S_.do("pe", lambda e: e.matmul(
                psum_c[:, :], onesrf[:, :], rowi[:, :], start=False, stop=True),
                wait=[t_rw, t_orf], tok=True)
            t_lw = S_.do("dve", lambda e, og=og: e.tensor_tensor(
                locw[:], psum_c[:, :], og[:, 0:16], ALU.mult), wait=[t_c3], tok=True)
            pr_guard[0] = t_lw
            t_loc = S_.do("dve", lambda e, mt=mt: e.reduce_sum(mt[:, 0:1], locw[:], axis=AX.X),
                          tok=True)
            S_.do("dve", lambda e, og=og: e.tensor_tensor(locw[:], og[:, 0:16], eiota[:],
                                                          ALU.mult), wait=[t_ei])
            t_et = S_.do("dve", lambda e, mt=mt: e.reduce_sum(mt[:, 1:2], locw[:], axis=AX.X),
                         tok=True)
            S_.do("dve", lambda e, mt=mt: e.tensor_scalar(mt[:, 2:3], mt[:, 0:1],
                                                          float(CAP), None, ALU.is_lt))
            S_.do("dve", lambda e, mt=mt: e.tensor_scalar(mt[:, 3:4], mt[:, 0:1],
                                                          float(CAP), None, ALU.min))
            S_.do("dve", lambda e, mt=mt: e.scalar_tensor_tensor(
                mt[:, 4:5], mt[:, 1:2], float(ER), mt[:, 3:4], ALU.mult, ALU.add))
            S_.do("dve", lambda e, mt=mt: e.tensor_scalar(
                mt[:, 5:6], mt[:, 1:2], ebase[:, 0:1], None, ALU.subtract),
                wait=[t_eb])
            S_.do("dve", lambda e, mt=mt: e.tensor_scalar(mt[:, 6:7], mt[:, 5:6], 0.0,
                                                          None, ALU.is_lt))
            S_.do("dve", lambda e, mt=mt: e.scalar_tensor_tensor(
                mt[:, 5:6], mt[:, 6:7], 16.0, mt[:, 5:6], ALU.mult, ALU.add))
            t_sl = S_.do("dve", lambda e, mt=mt: e.scalar_tensor_tensor(
                mt[:, 6:7], mt[:, 5:6], float(ER), mt[:, 3:4], ALU.mult, ALU.add),
                tok=True)
            t_si = S_.do("dve", lambda e, mt=mt, sI=sI: e.tensor_copy(sI[:], mt[:, 6:7]),
                         wait=[t_sl], tok=True)
            S_.do("dve", lambda e, mt=mt, og=og: e.tensor_copy(mt[:, 7:8], og[:, 16:17]))
            S_.do("dve", lambda e, mt=mt: e.tensor_copy(locw[:, 0:1], mt[:, 4:5]))
            S_.do("dve", lambda e, mt=mt: e.tensor_copy(locw[:, 1:2], mt[:, 7:8]))
            S_.do("dve", lambda e, mt=mt: e.tensor_copy(locw[:, 2:3], mt[:, 2:3]))
            t_rm2 = S_.do("dve", lambda e, mt=mt: e.tensor_copy(locw[:, 3:4], mt[:, 1:2]),
                          tok=True)
            t_rst = S_.do("sp", lambda e, i=i: e.dma_start(
                out=rmeta.ap()[i * 128:(i + 1) * 128, :], in_=locw[:, 0:4]),
                wait=[t_rm2], tok=True, dma=True)
            rm_guard[p] = t_rst
            t_sct = S_.do("pool", lambda e, xh=xh, sI=sI: e.indirect_dma_start(
                out=disp_d.ap()[0:NE * ER, :],
                out_offset=IndirectOffsetOnAxis(sI[:, 0:1], 0),
                in_=xh[:, :], in_offset=None),
                wait=[t_si, t_lx, z_tok], tok=True, dma=True)
            sc_guard[p] = t_sct
            xh_guard[p] = t_sct

        disp_done = [sc_guard[0], sc_guard[1], rm_guard[0], rm_guard[1]]

        p2 = ExitStack()
        w1sb = sb(p2, "w1sb", [128, 8 * FF], BF16)
        w2sb = sb(p2, "w2sb", [128, 32 * D], BF16)
        dT = sb(p2, "dT", [128, 8 * ER], BF16)
        dstg = sb(p2, "dstg", [128, 1024], BF16)
        hstg = sb(p2, "hstg", [128, 128], BF16)
        eost = [sb(p2, f"eost{i}", [128, 1024], F32) for i in range(2)]
        psum_h = p2.enter_context(nc.psum_tensor("psum_h", [128, 128], F32))
        psum_o = p2.enter_context(nc.psum_tensor("psum_o", [128, 1024], F32))

        eo_guard = [None, None]
        h_guard = [None]
        o_guard = [None]
        d_guard = [None]
        last_eo = [None]
        for le in range(2):
            t_w1 = S_.do("pool", lambda e, le=le: e.dma_start(
                out=w1sb[:, :].rearrange("p (c n) -> p c n", c=8),
                in_=w1_d.ap()[le * D:(le + 1) * D, :].rearrange("(c p) n -> p c n", p=128)),
                wait=[h_guard[0]], tok=True, dma=True)
            t_w2 = S_.do("pool", lambda e, le=le: e.dma_start(
                out=w2sb[:, :].rearrange("p (c n) -> p c n", c=32),
                in_=w2_d.ap()[le * FF:(le + 1) * FF, :].rearrange("(c p) n -> p c n", p=128)),
                wait=[o_guard[0]], tok=True, dma=True)
            dts = []
            dstg_guard = [None]
            for rt in range(ER // 128):
                t_dl = S_.do("sp", lambda e, le=le, rt=rt: e.dma_start(
                    out=dstg[:], in_=disp_d.ap()[le * ER + rt * 128: le * ER + (rt + 1) * 128, :]),
                    wait=disp_done + [d_guard[0], dstg_guard[0], h_guard[0]],
                    tok=True, dma=True)
                tk = emit_transpose_tile(
                    dstg, [t_dl], lambda wt, rt=rt: S_.do("dve", lambda e, rt=rt: e.tensor_copy(
                        dT[:, :].rearrange("p (c r) -> p c r", c=8)[:, :, rt * 128:(rt + 1) * 128],
                        psum_tr[:, :].rearrange("p (c t) -> p c t", c=8)), wait=wt, tok=True))
                dts.append(tk)
                dstg_guard[0] = tk
            t_dT = dts[-1]
            d_guard[0] = t_dT

            for rt in range(ER // 128):
                ost = eost[rt % 2]
                ocs = []
                for oc in range(32):
                    mts = None
                    for ic in range(8):
                        w = [t_dT, t_w1, h_guard[0]] if ic == 0 else []
                        mts = S_.do("pe", lambda e, oc=oc, ic=ic, rt=rt: e.matmul(
                            psum_h[:, :],
                            w1sb[:, ic * FF + oc * 128: ic * FF + (oc + 1) * 128],
                            dT[:, :].rearrange("p (c r) -> p c r", c=8)[:, ic,
                                                                       rt * 128:(rt + 1) * 128],
                            start=(ic == 0), stop=(ic == 7)), wait=w, tok=(ic == 7))
                    t_h = S_.do("act", lambda e: e.activation(hstg[:], psum_h[:, :], AF.Relu),
                                wait=[mts], tok=True)
                    h_guard[0] = t_h
                    mo = None
                    for dg in range(2):
                        w = [t_h, t_w2]
                        if oc == 0 and dg == 0:
                            w.append(o_guard[0])
                        mo = S_.do("pe", lambda e, oc=oc, dg=dg: e.matmul(
                            psum_o[:, dg * 512:(dg + 1) * 512], hstg[:, :],
                            w2sb[:, oc * D + dg * 512: oc * D + (dg + 1) * 512],
                            start=(oc == 0), stop=(oc == 31)), wait=w, tok=(dg == 1))
                    ocs.append(mo)
                t_ev = S_.do("dve", lambda e, ost=ost: e.tensor_copy(ost[:], psum_o[:, :]),
                             wait=[ocs[-1], eo_guard[rt % 2]], tok=True)
                o_guard[0] = t_ev
                t_st = S_.do("sp", lambda e, le=le, rt=rt, ost=ost: e.dma_start(
                    out=eo.ap()[le * ER + rt * 128: le * ER + (rt + 1) * 128, :], in_=ost[:]),
                    wait=[t_ev], tok=True, dma=True)
                eo_guard[rt % 2] = t_st
                last_eo[0] = t_st
        p2.close()
        S_.emit(nc, final_waits=[last_eo[0], rm_guard[0], rm_guard[1]])
    return nc


def build_ffn_b():
    nc = new_nc("ffnb")
    x_my = nc.dram_tensor("x_my", [T, D], F32, kind="ExternalInput")
    rmeta = nc.dram_tensor("rmeta", [T, 4], F32, kind="ExternalInput")
    eo_all = nc.dram_tensor("eo_all", [NE * ER, D], F32, kind="ExternalInput")
    xo = nc.dram_tensor("xo", [T, D], F32, kind="ExternalOutput")

    S_ = Sched()
    st = ExitStack()
    with st:
        def sb(name, shape, dt):
            return st.enter_context(nc.sbuf_tensor(name, shape, dt))
        xt = [sb(f"xt{i}", [128, 1024], F32) for i in range(2)]
        mt = [sb(f"mt{i}", [128, 4], F32) for i in range(2)]
        sI = [sb(f"sI{i}", [128, 1], I32) for i in range(2)]
        g = [sb(f"g{i}", [128, 1024], F32) for i in range(2)]
        sc = [sb(f"sc{i}", [128, 1], F32) for i in range(2)]
        xores = [sb(f"xores{i}", [128, 1024], F32) for i in range(2)]
        last = [None]
        guards = [None, None]
        for i in range(NMT):
            p = i % 2
            t_ld = S_.do("sp", lambda e, i=i, p=p: e.dma_start(
                out=xt[p][:], in_=x_my.ap()[i * 128:(i + 1) * 128, :]),
                wait=[guards[p]], tok=True, dma=True)
            t_lm = S_.do("sp", lambda e, i=i, p=p: e.dma_start(
                out=mt[p][:], in_=rmeta.ap()[i * 128:(i + 1) * 128, :]),
                wait=[guards[p]], tok=True, dma=True)
            t_si = S_.do("dve", lambda e, p=p: e.tensor_copy(sI[p][:], mt[p][:, 0:1]),
                         wait=[t_lm], tok=True)
            t_g = S_.do("pool", lambda e, p=p: e.indirect_dma_start(
                out=g[p][:, :], out_offset=None,
                in_=eo_all.ap()[0:NE * ER, :],
                in_offset=IndirectOffsetOnAxis(sI[p][:, 0:1], 0)),
                wait=[t_si, guards[p]], tok=True, dma=True)
            t_sc = S_.do("dve", lambda e, p=p: e.tensor_tensor(
                sc[p][:], mt[p][:, 1:2], mt[p][:, 2:3], ALU.mult), tok=True)
            t_gs = S_.do("dve", lambda e, p=p: e.tensor_scalar(
                g[p][:], g[p][:], sc[p][:, 0:1], None, ALU.mult), wait=[t_g, t_sc], tok=True)
            t_xo = S_.do("dve", lambda e, p=p: e.tensor_tensor(
                xores[p][:], g[p][:], xt[p][:], ALU.add), wait=[t_ld], tok=True)
            t_st = S_.do("sp", lambda e, i=i, p=p: e.dma_start(
                out=xo.ap()[i * 128:(i + 1) * 128, :], in_=xores[p][:]),
                wait=[t_xo], tok=True, dma=True)
            guards[p] = t_st
            last[0] = t_st
        S_.emit(nc, final_waits=[last[0]])
    return nc


def run_l0ffn(x, Wg, w1, w2):
    nca = _get("ffna", build_ffn_a)
    in_maps = []
    for c in range(N_CORES):
        in_maps.append({
            "x_all": x,
            "wg": Wg,
            "w1": np.ascontiguousarray(w1[2 * c:2 * c + 2].reshape(2 * D, FF)),
            "w2": np.ascontiguousarray(w2[2 * c:2 * c + 2].reshape(2 * FF, D)),
            "ebase": np.full((128, 1), 2.0 * c, dtype=np.float32),
        })
    res = run_bass_kernel_spmd(nca, in_maps, core_ids=list(range(N_CORES)))
    eo_all = np.concatenate([res.results[c]["eo"] for c in range(N_CORES)], axis=0)
    rmeta = res.results[0]["rmeta"]

    ncb = _get("ffnb", build_ffn_b)
    in_maps = []
    for c in range(N_CORES):
        in_maps.append({
            "x_my": x[c * T:(c + 1) * T],
            "rmeta": np.ascontiguousarray(rmeta[c * T:(c + 1) * T]),
            "eo_all": eo_all,
        })
    res = run_bass_kernel_spmd(ncb, in_maps, core_ids=list(range(N_CORES)))
    return np.concatenate([res.results[c]["xo"] for c in range(N_CORES)], axis=0)



MIDR = 256
MIDS = 512


def build_l1ffn(with_partials):
    nc = new_nc("l1ffnp" if with_partials else "l1ffn")
    x_my = nc.dram_tensor("x_my", [T, D], F32, kind="ExternalInput")
    parts = []
    if with_partials:
        parts = [nc.dram_tensor(f"part{i}", [T, D], F32, kind="ExternalInput")
                 for i in range(4)]
    rw_d = nc.dram_tensor("rw", [D, NE], F32, kind="ExternalInput")
    wgr_d = nc.dram_tensor("wgr", [NE * D, MIDR], F32, kind="ExternalInput")
    wur_d = nc.dram_tensor("wur", [NE * D, MIDR], F32, kind="ExternalInput")
    wdr_d = nc.dram_tensor("wdr", [NE * MIDR, D], F32, kind="ExternalInput")
    wgs_d = nc.dram_tensor("wgs", [2 * D, MIDS], F32, kind="ExternalInput")
    wus_d = nc.dram_tensor("wus", [2 * D, MIDS], F32, kind="ExternalInput")
    wds_d = nc.dram_tensor("wds", [2 * MIDS, D], F32, kind="ExternalInput")
    xo = nc.dram_tensor("xo", [T, D], F32, kind="ExternalOutput")
    xs_d = nc.dram_tensor("xs_d", [T, D], F32)

    i128c = nc.inline_tensor(np.eye(128, dtype=BF), name="ident128")
    i16fc = nc.inline_tensor(np.eye(16, dtype=np.float32), name="ident16f")

    S_ = Sched()
    st = ExitStack()
    with st:
        def sb(name, shape, dt):
            return st.enter_context(nc.sbuf_tensor(name, shape, dt))

        idn = sb("idn", [128, 128], BF16)
        idn16f = sb("idn16f", [16, 16], F32)
        rwf = sb("rwf", [128, 8 * NE], F32)
        rwh = sb("rwh", [128, 8 * NE], BF16)
        rwl = sb("rwl", [128, 8 * NE], BF16)
        xt = [sb(f"xt{i}", [128, 1024], F32) for i in range(2)]
        ptl = [sb("ptl0", [128, 1024], F32)] * 2
        stats = [sb(f"stats{i}", [128, 16], F32) for i in range(2)]
        sqln = sb("sqln", [128, 1024], BF16)
        rsc3 = sb("rsc3", [128, 48], F32)
        xhf = [sb(f"xhf{i}", [128, 1024], F32) for i in range(2)]
        xhat = [sb(f"xhat{i}", [128, 1024], BF16) for i in range(2)]
        xlo = [sb("xlo0", [128, 1024], BF16)] * 2
        xlT = [sb("xlT0", [128, 1024], BF16)] * 2
        xT_all = sb("xT_all", [128, 8 * 2048], BF16)
        lgT = sb("lgT", [16, 128], F32)
        lsb = sb("lsb", [128, NE], F32)
        rstat = sb("rstat", [128, 8], F32)
        eexp = sb("eexp", [128, NE], F32)
        comb_all = sb("comb_all", [128, 16 * NE], F32)
        acc = sb("acc", [128, 16 * 1024], F32)
        wga = sb("wga", [128, 16 * MIDS], BF16)
        wua = sb("wua", [128, 16 * MIDS], BF16)
        wda = sb("wda", [128, 8 * 1024], BF16)
        wgb = sb("wgb", [128, 8 * MIDR], BF16)
        wub = sb("wub", [128, 8 * MIDR], BF16)
        wdb = sb("wdb", [128, 2 * 1024], BF16)
        hst = sb("hst", [128, 512], BF16)
        hT = sb("hT", [128, 512], BF16)
        sg = sb("sg", [128, 512], BF16)
        xores = [sb(f"xores{i}", [128, 1024], F32) for i in range(2)]

        psum_tr = st.enter_context(nc.psum_tensor("psum_tr", [128, 1024], BF16))
        psum_r = st.enter_context(nc.psum_tensor("psum_r", [128, 512], F32))
        psum_g = st.enter_context(nc.psum_tensor("psum_g", [128, 512], F32))
        psum_u = st.enter_context(nc.psum_tensor("psum_u", [128, 512], F32))
        psum_o = st.enter_context(nc.psum_tensor("psum_o", [128, 1024], F32))

        t_id = S_.do("sp", lambda e: e.dma_start(out=idn[:], in_=i128c.ap()), tok=True, dma=True)
        t_i16f = S_.do("sp", lambda e: e.dma_start(out=idn16f[:], in_=i16fc.ap()), tok=True, dma=True)
        t_rwf = S_.do("sp", lambda e: e.dma_start(
            out=rwf[:, :].rearrange("p (c n) -> p c n", c=8),
            in_=rw_d.ap().rearrange("(c p) n -> p c n", p=128)), tok=True, dma=True)
        t_rh = S_.do("dve", lambda e: e.tensor_copy(rwh[:, :], rwf[:, :]),
                     wait=[t_rwf], tok=True)
        t_rw = S_.do("dve", lambda e: e.tensor_tensor(rwl[:, :], rwf[:, :], rwh[:, :],
                                                      ALU.subtract), tok=True)

        def emit_ln(xtb, stat, dst, wait):
            c = lambda k: stat[:, k:k + 1]
            t1 = S_.do("dve", lambda e: e.reduce_sum(c(0), xtb[:], axis=AX.X),
                       wait=wait, tok=True)
            t2 = S_.do("act", lambda e: e.activation(sqln[:], xtb[:], AF.Square,
                                                     accum_out=c(1)), wait=wait, tok=True)
            S_.do("dve", lambda e: e.tensor_scalar(c(2), c(0), 1.0 / D, None, ALU.mult),
                  wait=[t1])
            S_.do("dve", lambda e: e.tensor_scalar(c(3), c(1), 1.0 / D, None, ALU.mult),
                  wait=[t2])
            S_.do("dve", lambda e: e.tensor_tensor(c(4), c(2), c(2), ALU.mult))
            S_.do("dve", lambda e: e.tensor_tensor(c(5), c(3), c(4), ALU.subtract))
            t3 = S_.do("dve", lambda e: e.tensor_scalar(c(5), c(5), EPS_LN, None, ALU.add),
                       tok=True)
            S_.do("act", lambda e: e.activation(c(6), c(5), AF.Ln), wait=[t3])
            t4 = S_.do("act", lambda e: e.activation(c(6), c(6), AF.Exp, scale=-0.5),
                       tok=True)
            t5 = S_.do("dve", lambda e: e.scalar_tensor_tensor(
                c(7), c(2), -1.0, c(6), ALU.mult, ALU.mult), wait=[t4], tok=True)
            t6 = S_.do("act", lambda e: e.activation(
                dst[:], xtb[:], AF.Identity, bias=c(7), scale=c(6)), wait=[t5], tok=True)
            return t6

        tr_guard = [None]

        def emit_transpose_tile(src, wait, dst_emit, nch=8):
            tts = None
            for c in range(nch):
                w = (list(wait) + [tr_guard[0], t_id]) if c == 0 else []
                tts = S_.do("pe", lambda e, c=c, src=src: e.transpose(
                    psum_tr[:, c * 128:(c + 1) * 128], src[:, c * 128:(c + 1) * 128], idn[:]),
                    wait=w, tok=(c == nch - 1))
            t_ev = dst_emit([tts])
            tr_guard[0] = t_ev
            return t_ev

        xh_guard = [None, None]
        pr_guard = [None]
        ptl_guard = [None]
        for i in range(NMT):
            p = i % 2
            xtb, stat, xf, xh, xl, xlt = xt[p], stats[p], xhf[p], xhat[p], xlo[p], xlT[p]
            t_ld = S_.do("sp", lambda e, i=i, xtb=xtb: e.dma_start(
                out=xtb[:], in_=x_my.ap()[i * 128:(i + 1) * 128, :]),
                wait=[xh_guard[p]], tok=True, dma=True)
            if with_partials:
                t_ad = t_ld
                for pi in range(4):
                    t_lp = S_.do("sp", lambda e, i=i, pi=pi, p=p: e.dma_start(
                        out=ptl[p][:], in_=parts[pi].ap()[i * 128:(i + 1) * 128, :]),
                        wait=[t_ad, ptl_guard[0]], tok=True, dma=True)
                    t_ad = S_.do("dve", lambda e, p=p, xtb=xtb: e.tensor_tensor(
                        xtb[:], xtb[:], ptl[p][:], ALU.add), wait=[t_lp], tok=True)
                    ptl_guard[0] = t_ad
                t_ld = t_ad
            t_xs = S_.do("sp", lambda e, i=i, xtb=xtb: e.dma_start(
                out=xs_d.ap()[i * 128:(i + 1) * 128, :], in_=xtb[:]),
                wait=[t_ld], tok=True, dma=True)
            t_ln = emit_ln(xtb, stat, xf, [t_ld])
            t_hi = S_.do("dve", lambda e, xh=xh, xf=xf: e.tensor_copy(xh[:], xf[:]),
                         wait=[t_ln], tok=True)
            t_lo = S_.do("dve", lambda e, xh=xh, xf=xf, xl=xl: e.tensor_tensor(
                xl[:], xf[:], xh[:], ALU.subtract), tok=True)
            t_xT = emit_transpose_tile(xh, [t_hi], lambda wt, i=i: S_.do(
                "dve", lambda e, i=i: e.tensor_copy(
                    xT_all[:, :].rearrange("p (c t) -> p c t", c=8)[:, :, i * 128:(i + 1) * 128],
                    psum_tr[:, :].rearrange("p (c t) -> p c t", c=8)), wait=wt, tok=True))
            t_lT = emit_transpose_tile(xl, [t_lo], lambda wt, xlt=xlt: S_.do(
                "dve", lambda e, xlt=xlt: e.tensor_copy(xlt[:], psum_tr[:]), wait=wt, tok=True))
            mtoks = []
            for gi in range(4):
                wop = (rwh, rwl, rwh, rwl)[gi]
                use_lo = gi >= 2
                mts = None
                for ec in range(8):
                    w = [t_xT, t_lT, t_rw, pr_guard[0]] if (gi == 0 and ec == 0) else []
                    if use_lo:
                        def rhs_fn(ec=ec, xlt=xlt):
                            return xlt[:, ec * 128:(ec + 1) * 128]
                    else:
                        def rhs_fn(ec=ec, i=i):
                            return xT_all[:, ec * 2048 + i * 128: ec * 2048 + (i + 1) * 128]
                    mts = S_.do("pe", lambda e, gi=gi, wop=wop, ec=ec, rhs_fn=rhs_fn: e.matmul(
                        psum_r[0:16, gi * 128:(gi + 1) * 128],
                        wop[:, ec * NE:(ec + 1) * NE], rhs_fn(),
                        start=(ec == 0), stop=(ec == 7)), wait=w, tok=(ec == 7))
                mtoks.append(mts)
            S_.do("dve", lambda e: e.tensor_copy(lgT[:, :], psum_r[0:16, 0:128]),
                  wait=mtoks)
            S_.do("dve", lambda e: e.tensor_tensor(lgT[:, :], lgT[:, :],
                                                   psum_r[0:16, 128:256], ALU.add))
            S_.do("dve", lambda e: e.tensor_tensor(lgT[:, :], lgT[:, :],
                                                   psum_r[0:16, 256:384], ALU.add))
            t_lgT = S_.do("dve", lambda e: e.tensor_tensor(
                lgT[:, :], lgT[:, :], psum_r[0:16, 384:512], ALU.add), tok=True)
            t_ltr = S_.do("pe", lambda e: e.transpose(psum_r[:, 128:144], lgT[:, :],
                                                      idn16f[:]),
                          wait=[t_lgT, t_i16f], tok=True)
            t_l = S_.do("dve", lambda e: e.tensor_copy(lsb[:], psum_r[:, 128:144]),
                        wait=[t_ltr], tok=True)
            pr_guard[0] = t_l
            t_m = S_.do("dve", lambda e: e.reduce_max(rstat[:, 0:1], lsb[:], axis=AX.X),
                        tok=True)
            t_neg = S_.do("dve", lambda e: e.tensor_scalar(rstat[:, 1:2], rstat[:, 0:1],
                                                           -1.0, None, ALU.mult), tok=True)
            t_ex = S_.do("act", lambda e: e.activation(
                eexp[:], lsb[:], AF.Exp, bias=rstat[:, 1:2], accum_out=rstat[:, 2:3]),
                wait=[t_neg], tok=True)
            t_m1 = S_.do("dve", lambda e: e.reduce_max(rstat[:, 3:4], eexp[:], axis=AX.X),
                         wait=[t_ex], tok=True)
            S_.do("dve", lambda e: e.tensor_scalar(
                rsc3[:, 0:16], eexp[:], rstat[:, 3:4], None, ALU.is_ge))
            S_.do("dve", lambda e: e.tensor_tensor(rsc3[:, 16:32], rsc3[:, 0:16],
                                                   eexp[:], ALU.mult))
            S_.do("dve", lambda e: e.tensor_tensor(
                rsc3[:, 16:32], eexp[:], rsc3[:, 16:32], ALU.subtract))
            t_m2 = S_.do("dve", lambda e: e.reduce_max(rstat[:, 4:5], rsc3[:, 16:32],
                                                       axis=AX.X), tok=True)
            S_.do("dve", lambda e: e.tensor_scalar(
                rsc3[:, 32:48], rsc3[:, 16:32], rstat[:, 4:5], None, ALU.is_ge))
            S_.do("dve", lambda e: e.scalar_tensor_tensor(
                rstat[:, 5:6], rstat[:, 2:3], 1e-9, rstat[:, 4:5], ALU.mult, ALU.add))
            S_.do("dve", lambda e: e.tensor_tensor(
                rstat[:, 5:6], rstat[:, 5:6], rstat[:, 3:4], ALU.add))
            t_rc = S_.do("dve", lambda e: e.reciprocal(rstat[:, 6:7], rstat[:, 5:6]),
                         tok=True)
            S_.do("dve", lambda e: e.tensor_scalar(
                rsc3[:, 0:16], rsc3[:, 0:16], rstat[:, 3:4], None, ALU.mult))
            S_.do("dve", lambda e: e.tensor_scalar(
                rsc3[:, 32:48], rsc3[:, 32:48], rstat[:, 4:5], None, ALU.mult))
            S_.do("dve", lambda e: e.tensor_tensor(
                rsc3[:, 0:16], rsc3[:, 0:16], rsc3[:, 32:48], ALU.add))
            t_cb = S_.do("dve", lambda e, i=i: e.tensor_scalar(
                comb_all[:, i * NE:(i + 1) * NE], rsc3[:, 0:16], rstat[:, 6:7], None,
                ALU.mult), wait=[t_rc], tok=True)
            xh_guard[p] = t_cb

        p1_done = [xh_guard[0], xh_guard[1]]

        t_z = None
        for i in range(NMT):
            t_z = S_.do("dve", lambda e, i=i: e.memset(acc[:, i * 1024:(i + 1) * 1024], 0.0),
                        tok=True)

        def emit_glu_block(wg_, wu_, wd_, mid, comb_col, i, waits, into_acc, acc_first):
            nch = mid // 128
            mg = mu = None
            for ec in range(8):
                w = list(waits) if ec == 0 else []
                mg = S_.do("pe", lambda e, ec=ec, i=i, wg_=wg_, mid=mid: e.matmul(
                    psum_g[:, 0:mid],
                    xT_all[:, ec * 2048 + i * 128: ec * 2048 + (i + 1) * 128],
                    wg_[:, ec * mid:(ec + 1) * mid],
                    start=(ec == 0), stop=(ec == 7)), wait=w, tok=(ec == 7))
            for ec in range(8):
                mu = S_.do("pe", lambda e, ec=ec, i=i, wu_=wu_, mid=mid: e.matmul(
                    psum_u[:, 0:mid],
                    xT_all[:, ec * 2048 + i * 128: ec * 2048 + (i + 1) * 128],
                    wu_[:, ec * mid:(ec + 1) * mid],
                    start=(ec == 0), stop=(ec == 7)), wait=[], tok=(ec == 7))
            t_sg = S_.do("act", lambda e, mid=mid: e.activation(
                sg[:, 0:mid], psum_g[:, 0:mid], AF.Silu), wait=[mg], tok=True)
            t_h = S_.do("dve", lambda e, mid=mid: e.tensor_tensor(
                hst[:, 0:mid], psum_u[:, 0:mid], sg[:, 0:mid], ALU.mult),
                wait=[mu, t_sg], tok=True)
            if comb_col is not None:
                t_h = S_.do("dve", lambda e, mid=mid, comb_col=comb_col: e.tensor_scalar(
                    hst[:, 0:mid], hst[:, 0:mid], comb_col, None, ALU.mult), tok=True)
            tts = None
            for cc in range(nch):
                w = [t_h, tr_guard[0]] if cc == 0 else []
                tts = S_.do("pe", lambda e, cc=cc: e.transpose(
                    psum_tr[:, cc * 128:(cc + 1) * 128],
                    hst[:, cc * 128:(cc + 1) * 128], idn[:]), wait=w, tok=(cc == nch - 1))
            t_hT = S_.do("dve", lambda e, mid=mid: e.tensor_copy(hT[:, 0:mid],
                                                                 psum_tr[:, 0:mid]),
                         wait=[tts], tok=True)
            tr_guard[0] = t_hT
            md = None
            for dg in range(2):
                for cc in range(nch):
                    w = [t_hT]
                    if dg == 0 and cc == 0:
                        w.append(po_guard[0])
                    md = S_.do("pe", lambda e, cc=cc, dg=dg, wd_=wd_: e.matmul(
                        psum_o[:, dg * 512:(dg + 1) * 512],
                        hT[:, cc * 128:(cc + 1) * 128],
                        wd_[:, cc * 1024 + dg * 512: cc * 1024 + (dg + 1) * 512],
                        start=(cc == 0 and acc_first), stop=(cc == nch - 1 and into_acc)),
                        wait=w, tok=(dg == 1 and cc == nch - 1))
            return md, t_h

        wl_guard = [None]
        h_guard = [p1_done[0]]
        acc_guard = [t_z]
        po_guard = [None]
        for ei in range(NE):
            pw = ei % 2
            wg_, wu_, wd_ = (wga, wua, wda) if pw == 0 else (wgb, wub, wdb)
            t_wg = S_.do("pool", lambda e, ei=ei, wg_=wg_: e.dma_start(
                out=wg_[:, 0:8 * MIDR].rearrange("p (c n) -> p c n", c=8),
                in_=wgr_d.ap()[ei * D:(ei + 1) * D, :].rearrange("(c p) n -> p c n", p=128)),
                wait=[wl_guard[0]], tok=True, dma=True)
            t_wu = S_.do("pool", lambda e, ei=ei, wu_=wu_: e.dma_start(
                out=wu_[:, 0:8 * MIDR].rearrange("p (c n) -> p c n", c=8),
                in_=wur_d.ap()[ei * D:(ei + 1) * D, :].rearrange("(c p) n -> p c n", p=128)),
                tok=True, dma=True)
            t_wd = S_.do("pool", lambda e, ei=ei, wd_=wd_: e.dma_start(
                out=wd_[:, 0:2 * 1024].rearrange("p (c n) -> p c n", c=2),
                in_=wdr_d.ap()[ei * MIDR:(ei + 1) * MIDR, :].rearrange("(c p) n -> p c n", p=128)),
                tok=True, dma=True)
            last = None
            for i in range(NMT):
                md, t_h = emit_glu_block(
                    wg_, wu_, wd_, MIDR, comb_all[:, i * NE + ei:i * NE + ei + 1], i,
                    [t_wg, t_wu, t_wd, p1_done[0], p1_done[1], acc_guard[0]],
                    into_acc=True, acc_first=True)
                t_ac = S_.do("dve", lambda e, i=i: e.tensor_tensor(
                    acc[:, i * 1024:(i + 1) * 1024], acc[:, i * 1024:(i + 1) * 1024],
                    psum_o[:, 0:1024], ALU.add), wait=[md], tok=True)
                acc_guard[0] = t_ac
                po_guard[0] = t_ac
                last = t_ac
            wl_guard[0] = last

        t_sw = []
        for (dst, srcd, rows) in ((wga, wgs_d, 2 * D), (wua, wus_d, 2 * D)):
            t_sw.append(S_.do("pool", lambda e, dst=dst, srcd=srcd: e.dma_start(
                out=dst[:, 0:(rows // 128) * MIDS].rearrange("p (c n) -> p c n", c=rows // 128),
                in_=srcd.ap().rearrange("(c p) n -> p c n", p=128)),
                wait=[wl_guard[0]], tok=True, dma=True))
        t_sw.append(S_.do("pool", lambda e: e.dma_start(
            out=wda[:, 0:8 * 1024].rearrange("p (c n) -> p c n", c=8),
            in_=wds_d.ap().rearrange("(c p) n -> p c n", p=128)),
            wait=[wl_guard[0]], tok=True, dma=True))
        xo_guard = [None, None]
        last_store = [None]
        for i in range(NMT):
            p = i % 2
            md = None
            for se in range(2):
                nch = MIDS // 128
                mg = mu = None
                for ec in range(8):
                    w = list(t_sw) + [wl_guard[0]] if (se == 0 and ec == 0) else []
                    mg = S_.do("pe", lambda e, ec=ec, i=i, se=se: e.matmul(
                        psum_g[:, 0:MIDS],
                        xT_all[:, ec * 2048 + i * 128: ec * 2048 + (i + 1) * 128],
                        wga[:, (se * 8 + ec) * MIDS:(se * 8 + ec + 1) * MIDS],
                        start=(ec == 0), stop=(ec == 7)), wait=w, tok=(ec == 7))
                for ec in range(8):
                    mu = S_.do("pe", lambda e, ec=ec, i=i, se=se: e.matmul(
                        psum_u[:, 0:MIDS],
                        xT_all[:, ec * 2048 + i * 128: ec * 2048 + (i + 1) * 128],
                        wua[:, (se * 8 + ec) * MIDS:(se * 8 + ec + 1) * MIDS],
                        start=(ec == 0), stop=(ec == 7)), wait=[], tok=(ec == 7))
                t_sg = S_.do("act", lambda e: e.activation(
                    sg[:, 0:MIDS], psum_g[:, 0:MIDS], AF.Silu), wait=[mg], tok=True)
                t_h = S_.do("dve", lambda e: e.tensor_tensor(
                    hst[:, 0:MIDS], psum_u[:, 0:MIDS], sg[:, 0:MIDS], ALU.mult),
                    wait=[mu, t_sg], tok=True)
                tts = None
                for cc in range(nch):
                    w = [t_h, tr_guard[0]] if cc == 0 else []
                    tts = S_.do("pe", lambda e, cc=cc: e.transpose(
                        psum_tr[:, cc * 128:(cc + 1) * 128],
                        hst[:, cc * 128:(cc + 1) * 128], idn[:]), wait=w, tok=(cc == nch - 1))
                t_hT = S_.do("dve", lambda e: e.tensor_copy(hT[:, 0:MIDS],
                                                            psum_tr[:, 0:MIDS]),
                             wait=[tts], tok=True)
                tr_guard[0] = t_hT
                for dg in range(2):
                    for cc in range(nch):
                        w = [t_hT]
                        if se == 0 and dg == 0 and cc == 0:
                            w.append(po_guard[0])
                        md = S_.do("pe", lambda e, cc=cc, dg=dg, se=se: e.matmul(
                            psum_o[:, dg * 512:(dg + 1) * 512],
                            hT[:, cc * 128:(cc + 1) * 128],
                            wda[:, (se * nch + cc) * 1024 + dg * 512:
                                 (se * nch + cc) * 1024 + (dg + 1) * 512],
                            start=(cc == 0 and se == 0), stop=(cc == nch - 1 and se == 1)),
                            wait=w, tok=(dg == 1 and cc == nch - 1 and se == 1))
            t_xl = S_.do("sp", lambda e, i=i, p=p: e.dma_start(
                out=xt[p][:], in_=xs_d.ap()[i * 128:(i + 1) * 128, :]),
                wait=[xo_guard[p]], tok=True, dma=True)
            t_e1 = S_.do("dve", lambda e, i=i, p=p: e.tensor_tensor(
                xores[p][:], psum_o[:, 0:1024], acc[:, i * 1024:(i + 1) * 1024], ALU.add),
                wait=[md, wl_guard[0], xo_guard[p]], tok=True)
            po_guard[0] = t_e1
            t_e2 = S_.do("dve", lambda e, p=p: e.tensor_tensor(
                xores[p][:], xores[p][:], xt[p][:], ALU.add), wait=[t_xl], tok=True)
            t_st = S_.do("sp", lambda e, i=i, p=p: e.dma_start(
                out=xo.ap()[i * 128:(i + 1) * 128, :], in_=xores[p][:]),
                wait=[t_e2], tok=True, dma=True)
            xo_guard[p] = t_st
            last_store[0] = t_st
        S_.emit(nc, final_waits=[last_store[0]])
    return nc


def run_l1ffn(x, router_w, wg_r, wu_r, wd_r, wg_s, wu_s, wd_s, partials=None):
    key = "l1ffnp" if partials is not None else "l1ffn"
    nc = _get(key, lambda: build_l1ffn(partials is not None))
    in_maps = []
    for c in range(N_CORES):
        m = {
            "x_my": x[c * T:(c + 1) * T],
            "rw": router_w,
            "wgr": np.ascontiguousarray(wg_r.reshape(NE * D, MIDR)),
            "wur": np.ascontiguousarray(wu_r.reshape(NE * D, MIDR)),
            "wdr": np.ascontiguousarray(wd_r.reshape(NE * MIDR, D)),
            "wgs": np.ascontiguousarray(wg_s.reshape(2 * D, MIDS)),
            "wus": np.ascontiguousarray(wu_s.reshape(2 * D, MIDS)),
            "wds": np.ascontiguousarray(wd_s.reshape(2 * MIDS, D)),
        }
        if partials is not None:
            for i in range(4):
                m[f"part{i}"] = np.ascontiguousarray(partials[i][c * T:(c + 1) * T])
        in_maps.append(m)
    res = run_bass_kernel_spmd(nc, in_maps, core_ids=list(range(N_CORES)))
    return np.concatenate([res.results[c]["xo"] for c in range(N_CORES)], axis=0)



def build_l1a():
    nc = new_nc("l1a")
    x_b = nc.dram_tensor("x_b", [S, D], F32, kind="ExternalInput")
    wq_d = nc.dram_tensor("wq", [D, 256], F32, kind="ExternalInput")
    wk_d = nc.dram_tensor("wk", [D, 256], F32, kind="ExternalInput")
    wv_d = nc.dram_tensor("wv", [D, 256], F32, kind="ExternalInput")
    wo_d = nc.dram_tensor("wo", [256, D], F32, kind="ExternalInput")
    cosi_d = nc.dram_tensor("cosi", [S, 128], F32, kind="ExternalInput")
    sini_d = nc.dram_tensor("sini", [S, 128], F32, kind="ExternalInput")
    part = nc.dram_tensor("part", [S, D], F32, kind="ExternalOutput")

    i128c = nc.inline_tensor(np.eye(128, dtype=BF), name="ident128")
    i2c = nc.inline_tensor(np.eye(2, dtype=BF), name="ident2")
    ones1fc = nc.inline_tensor(np.ones((1, 128), dtype=np.float32), name="ones1f")
    ones1bc = nc.inline_tensor(np.ones((1, 128), dtype=BF), name="ones1b")

    S_ = Sched()
    st = ExitStack()
    with st:
        def sb(name, shape, dt):
            return st.enter_context(nc.sbuf_tensor(name, shape, dt))

        idn = sb("idn", [128, 128], BF16)
        idn2 = sb("idn2", [2, 2], BF16)
        onesrf = sb("onesrf", [1, 128], F32)
        onesrb = sb("onesrb", [1, 128], BF16)
        wq = sb("wq_s", [128, 8 * 256], BF16)
        wk = sb("wk_s", [128, 8 * 256], BF16)
        wv = sb("wv_s", [128, 8 * 256], BF16)
        wo = sb("wo_s", [128, 2 * 1024], BF16)
        xt = [sb(f"xt{i}", [128, 1024], F32) for i in range(2)]
        stats = [sb(f"stats{i}", [128, 16], F32) for i in range(2)]
        sqln = sb("sqln", [128, 1024], BF16)
        sqkv = sb("sqkv", [128, 128], BF16)
        xhat = [sb(f"xhat{i}", [128, 1024], BF16) for i in range(2)]
        xhTt = [sb(f"xhTt{i}", [128, 1024], BF16) for i in range(2)]
        qkv = sb("qkv_s", [128, 384], BF16)
        ropes = sb("ropes", [128, 256], F32)
        rt1 = sb("rt1", [128, 64], F32)
        rt2 = sb("rt2", [128, 64], F32)
        rt3 = sb("rt3", [128, 64], F32)
        rt4 = sb("rt4", [128, 64], F32)
        n2t = sb("n2t", [128, 2], F32)
        n2tb = sb("n2tb", [128, 2], BF16)
        nm2 = sb("nm2", [2, 1], BF16)
        nkmax = sb("nkmax", [1, 2], F32)
        nkb = sb("nkb", [1, 2], BF16)
        mkb = sb("mkb", [128, 2], F32)
        ctile = sb("ctile", [128, 2], BF16)
        n2q_all = sb("n2q_all", [128, 2 * NBT], F32)
        qtil = [sb(f"qtil{h}", [65, S], BF16) for h in range(2)]
        ktil = [sb(f"ktil{h}", [65, S], BF16) for h in range(2)]
        vtil = [sb(f"vtil{h}", [128, 64 * 65], BF16) for h in range(2)]
        aT = [sb(f"aT{i}", [128, S], BF16) for i in range(2)]
        pt = sb("pt_s", [128, 2 * 1024], BF16)
        rp = sb("rp", [1, 512], F32)
        bb = sb("bb", [64, 512], F32)
        xores = [sb(f"xores{i}", [128, 1024], F32) for i in range(2)]

        psum_tr = st.enter_context(nc.psum_tensor("psum_tr", [128, 1024], BF16))

        t_id = S_.do("sp", lambda e: e.dma_start(out=idn[:], in_=i128c.ap()), tok=True, dma=True)
        t_id2 = S_.do("sp", lambda e: e.dma_start(out=idn2[:], in_=i2c.ap()), tok=True, dma=True)
        t_orf = S_.do("sp", lambda e: e.dma_start(out=onesrf[:], in_=ones1fc.ap()), tok=True, dma=True)
        t_orb = S_.do("sp", lambda e: e.dma_start(out=onesrb[:], in_=ones1bc.ap()), tok=True, dma=True)

        def load_w(dst, src_dram, wait=()):
            K, N = src_dram.shape
            nchunk = K // 128
            ap = src_dram.ap().rearrange("(c p) n -> p c n", p=128)
            dstap = dst[:, 0:nchunk * N].rearrange("p (c n) -> p c n", c=nchunk)
            return S_.do("pool", lambda e: e.dma_start(out=dstap, in_=ap),
                         wait=wait, tok=True, dma=True)

        t_wq = load_w(wq, wq_d)
        t_wk = load_w(wk, wk_d)
        t_wv = load_w(wv, wv_d)
        t_wo = load_w(wo, wo_d)

        def emit_ln(xtb, stat, dst, wait):
            c = lambda k: stat[:, k:k + 1]
            t1 = S_.do("dve", lambda e: e.reduce_sum(c(0), xtb[:], axis=AX.X),
                       wait=wait, tok=True)
            t2 = S_.do("act", lambda e: e.activation(sqln[:], xtb[:], AF.Square,
                                                     accum_out=c(1)), wait=wait, tok=True)
            S_.do("dve", lambda e: e.tensor_scalar(c(2), c(0), 1.0 / D, None, ALU.mult),
                  wait=[t1])
            S_.do("dve", lambda e: e.tensor_scalar(c(3), c(1), 1.0 / D, None, ALU.mult),
                  wait=[t2])
            S_.do("dve", lambda e: e.tensor_tensor(c(4), c(2), c(2), ALU.mult))
            S_.do("dve", lambda e: e.tensor_tensor(c(5), c(3), c(4), ALU.subtract))
            t3 = S_.do("dve", lambda e: e.tensor_scalar(c(5), c(5), EPS_LN, None, ALU.add),
                       tok=True)
            S_.do("act", lambda e: e.activation(c(6), c(5), AF.Ln), wait=[t3])
            t4 = S_.do("act", lambda e: e.activation(c(6), c(6), AF.Exp, scale=-0.5),
                       tok=True)
            t5 = S_.do("dve", lambda e: e.scalar_tensor_tensor(
                c(7), c(2), -1.0, c(6), ALU.mult, ALU.mult), wait=[t4], tok=True)
            t6 = S_.do("act", lambda e: e.activation(
                dst[:], xtb[:], AF.Identity, bias=c(7), scale=c(6)), wait=[t5], tok=True)
            return t6

        tr_guard = [None]

        def emit_transpose_tile(src, wait, dst_emit, nch=8):
            tts = None
            for c in range(nch):
                w = (list(wait) + [tr_guard[0], t_id]) if c == 0 else []
                tts = S_.do("pe", lambda e, c=c, src=src: e.transpose(
                    psum_tr[:, c * 128:(c + 1) * 128], src[:, c * 128:(c + 1) * 128], idn[:]),
                    wait=w, tok=(c == nch - 1))
            t_ev = dst_emit([tts])
            tr_guard[0] = t_ev
            return t_ev

        def emit_rope_i(tile, col0, i, wait, pair):
            t_c = S_.do("sp", lambda e, i=i, pair=pair: e.dma_start(
                out=ropes[:, 0:64],
                in_=cosi_d.ap()[i * 128:(i + 1) * 128, pair * 64:pair * 64 + 64]),
                wait=wait, tok=True, dma=True)
            t_s = S_.do("sp", lambda e, i=i, pair=pair: e.dma_start(
                out=ropes[:, 128:192],
                in_=sini_d.ap()[i * 128:(i + 1) * 128, pair * 64:pair * 64 + 64]),
                wait=wait, tok=True, dma=True)
            x1 = tile[:, col0:col0 + 128].rearrange("p (hk two) -> p hk two",
                                                    two=2)[:, :, 0:1]
            x2 = tile[:, col0:col0 + 128].rearrange("p (hk two) -> p hk two",
                                                    two=2)[:, :, 1:2]
            cos = ropes[:, 0:64].rearrange("p (n o) -> p n o", o=1)
            sin = ropes[:, 128:192].rearrange("p (n o) -> p n o", o=1)
            r2 = lambda t: t[:, :].rearrange("p (n o) -> p n o", o=1)
            ta = S_.do("dve", lambda e: e.tensor_tensor(r2(rt1), x1, cos, ALU.mult),
                       wait=[t_c] + list(wait), tok=True)
            S_.do("dve", lambda e: e.tensor_tensor(r2(rt2), x2, sin, ALU.mult), wait=[t_s])
            S_.do("dve", lambda e: e.tensor_tensor(r2(rt3), x2, cos, ALU.mult))
            S_.do("dve", lambda e: e.tensor_tensor(r2(rt4), x1, sin, ALU.mult))
            S_.do("dve", lambda e: e.tensor_tensor(x1, r2(rt1), r2(rt2), ALU.subtract))
            tf = S_.do("dve", lambda e: e.tensor_tensor(x2, r2(rt3), r2(rt4), ALU.add),
                       tok=True)
            return tf


        pn2_guard = [None]
        pn1_guard = [None]
        nm2_guard = [None]
        NKT = S // 128

        part_done = []
        ph_stack = [None]
        for pair in range(2):
            p1ps = ExitStack()
            psum_p = p1ps.enter_context(nc.psum_tensor(f"psum_p{pair}", [128, 512], F32))
            psum_h2 = p1ps.enter_context(nc.psum_tensor(f"psum_h2{pair}", [2, 128], BF16))
            psum_h1 = p1ps.enter_context(nc.psum_tensor(f"psum_h1{pair}", [1, 128], BF16))
            t_nk0 = S_.do("dve", lambda e: e.memset(nkmax[:], 0.0), tok=True)
            t_k1 = [S_.do("dve", lambda e, h=h: e.memset(ktil[h][64:65, :], 1.0), tok=True)
                    for h in range(2)]
            t_v1 = [S_.do("dve", lambda e, h=h: e.memset(
                vtil[h][:, :].rearrange("p (k c) -> p k c", c=65)[:, :, 64:65], 1.0),
                tok=True) for h in range(2)]
            xh_guard = [None, None]
            pr_guard = [None]
            qkv_guard = [None]
            c_done = []
            for i in range(NBT):
                p = i % 2
                xtb, stat, xh, xT = xt[p], stats[p], xhat[p], xhTt[p]
                t_ld = S_.do("sp", lambda e, i=i, xtb=xtb: e.dma_start(
                    out=xtb[:], in_=x_b.ap()[i * 128:(i + 1) * 128, :]),
                    wait=[xh_guard[p]], tok=True, dma=True)
                t_ln = emit_ln(xtb, stat, xh, [t_ld])
                t_xT = emit_transpose_tile(xh, [t_ln], lambda wt, xT=xT: S_.do(
                    "dve", lambda e: e.tensor_copy(xT[:], psum_tr[:]), wait=wt, tok=True))
                xh_guard[p] = t_xT
                mts = None
                for gi, (wbuf, wtk) in enumerate(((wq, t_wq), (wk, t_wk), (wv, t_wv))):
                    for ec in range(8):
                        w = [t_xT, wtk, pr_guard[0]] if ec == 0 else []
                        mts = S_.do("pe", lambda e, gi=gi, ec=ec, wbuf=wbuf, xT=xT,
                                    pair=pair: e.matmul(
                            psum_p[:, gi * 128:(gi + 1) * 128],
                            xT[:, ec * 128:(ec + 1) * 128],
                            wbuf[:, ec * 256 + pair * 128: ec * 256 + (pair + 1) * 128],
                            start=(ec == 0), stop=(ec == 7)), wait=w, tok=(ec == 7))
                t_qkv = S_.do("dve", lambda e: e.tensor_copy(qkv[:], psum_p[:, 0:384]),
                              wait=[mts, qkv_guard[0]], tok=True)
                pr_guard[0] = t_qkv
                t_rq = emit_rope_i(qkv, 0, i, [t_qkv], pair)
                t_rk = emit_rope_i(qkv, 128, i, [t_rq], pair)
                t_ks = S_.do("dve", lambda e: e.tensor_scalar(
                    qkv[:, 128:256], qkv[:, 128:256], 0.125, None, ALU.mult), tok=True)
                t_sq = S_.do("dve", lambda e: e.tensor_tensor(
                    sqkv[:], qkv[:, 128:256], qkv[:, 128:256], ALU.mult), tok=True)
                t_n2 = S_.do("dve", lambda e: e.tensor_reduce(
                    n2t[:], sqkv[:].rearrange("p (h k) -> p h k", k=64), AX.X, ALU.add),
                    tok=True)
                t_n2b = S_.do("dve", lambda e: e.tensor_copy(n2tb[:], n2t[:]), tok=True)
                t_t1 = S_.do("pe", lambda e: e.transpose(psum_h2[:, :], n2tb[:], idn[:]),
                             wait=[t_n2b, pn2_guard[0], t_id], tok=True)
                t_r = S_.do("dve", lambda e: e.reduce_max(nm2[:, 0:1], psum_h2[:, :],
                                                          axis=AX.X),
                            wait=[t_t1, nm2_guard[0]], tok=True)
                pn2_guard[0] = t_r
                t_t2 = S_.do("pe", lambda e: e.transpose(psum_h1[0:1, 0:2], nm2[:, 0:1],
                                                         idn2[:]),
                             wait=[t_r, pn1_guard[0], t_id2], tok=True)
                nm2_guard[0] = t_t2
                t_mx = S_.do("dve", lambda e: e.tensor_tensor(
                    nkmax[:], nkmax[:], psum_h1[0:1, 0:2], ALU.max),
                    wait=[t_t2, t_nk0], tok=True)
                pn1_guard[0] = t_mx
                t_sq2 = S_.do("dve", lambda e: e.tensor_tensor(
                    sqkv[:], qkv[:, 0:128], qkv[:, 0:128], ALU.mult), tok=True)
                t_n2q = S_.do("dve", lambda e, i=i: e.tensor_reduce(
                    n2q_all[:, i * 2:(i + 1) * 2],
                    sqkv[:].rearrange("p (h k) -> p h k", k=64), AX.X, ALU.add), tok=True)
                def dst_qk(wt, i=i):
                    last = None
                    for hh in range(2):
                        S_.do("dve", lambda e, hh=hh, i=i: e.tensor_copy(
                            qtil[hh][0:64, i * 128:(i + 1) * 128],
                            psum_tr[hh * 64:(hh + 1) * 64, 0:128]),
                            wait=(wt if hh == 0 else []))
                        last = S_.do("dve", lambda e, hh=hh, i=i: e.tensor_copy(
                            ktil[hh][0:64, i * 128:(i + 1) * 128],
                            psum_tr[hh * 64:(hh + 1) * 64, 128:256]), tok=(hh == 1))
                    return last
                tts = None
                for cc in range(2):
                    w = [t_rk, t_ks, t_sq, t_sq2, tr_guard[0], t_k1[0], t_k1[1]] if cc == 0 else []
                    tts = S_.do("pe", lambda e, cc=cc: e.transpose(
                        psum_tr[:, cc * 128:(cc + 1) * 128],
                        qkv[:, cc * 128:(cc + 1) * 128], idn[:]), wait=w, tok=(cc == 1))
                t_qk = dst_qk([tts])
                tr_guard[0] = t_qk
                t_vc = None
                for hh in range(2):
                    t_vc = S_.do("dve", lambda e, hh=hh, i=i: e.tensor_copy(
                        vtil[hh][:, i * 65:i * 65 + 64],
                        qkv[:, 256 + hh * 64:256 + (hh + 1) * 64]),
                        wait=([t_v1[hh]] if i == 0 else []), tok=(hh == 1))
                qkv_guard[0] = t_vc
            t_nkb = S_.do("dve", lambda e: e.tensor_copy(nkb[:], nkmax[:]),
                          wait=[pn1_guard[0]], tok=True)
            t_mkbm = S_.do("pe", lambda e: e.matmul(psum_p[:, 384:386], onesrb[:, :],
                                                    nkb[0:1, :], start=True, stop=True),
                           wait=[t_nkb, t_orb, pr_guard[0]], tok=True)
            t_mkb = S_.do("dve", lambda e: e.tensor_copy(mkb[:], psum_p[:, 384:386]),
                          wait=[t_mkbm], tok=True)
            c_last = None
            for i in range(NBT):
                t_c1 = S_.do("dve", lambda e, i=i: e.tensor_tensor(
                    ctile[:, 0:2], n2q_all[:, i * 2:(i + 1) * 2], mkb[:], ALU.mult),
                    wait=[t_mkb], tok=True)
                t_c2 = S_.do("act", lambda e: e.activation(ctile[:], ctile[:], AF.Sqrt,
                                                           scale=1.1025),
                             wait=[t_c1], tok=True)
                for hh in range(2):
                    t_tc = S_.do("pe", lambda e, hh=hh: e.transpose(
                        psum_h1[0:1, :], ctile[:, hh:hh + 1], idn[:]),
                        wait=[t_c2, pn1_guard[0], t_id], tok=True)
                    t_wr = S_.do("dve", lambda e, hh=hh, i=i: e.tensor_scalar(
                        qtil[hh][64:65, i * 128:(i + 1) * 128], psum_h1[0:1, :],
                        -1.0, None, ALU.mult), wait=[t_tc], tok=True)
                    pn1_guard[0] = t_wr
                c_last = t_wr
            q_done = c_last
            p1ps.close()

            flps = ExitStack()
            psum_s = [flps.enter_context(nc.psum_tensor(f"psum_s{pair}_{i}", [128, 1024], F32))
                      for i in range(2)]
            psum_av = flps.enter_context(nc.psum_tensor(f"psum_av{pair}", [65, 512], F32))
            psum_bc = flps.enter_context(nc.psum_tensor(f"psum_bc{pair}", [64, 512], F32))
            s_guard = [None, None]
            pt_guard = [None, None]
            av_guard = [q_done]
            bc_guard = [None]
            aT_last = None
            for hh in range(2):
                for qg in range(16):
                    nkts = 4 * (qg + 1)
                    ngrp = (nkts + 1) // 2
                    last_av = None
                    for g in range(ngrp):
                        par = g % 2
                        kts = [kt for kt in (2 * g, 2 * g + 1) if kt < nkts]
                        mts = None
                        for j, kt in enumerate(kts):
                            w = [q_done, s_guard[par]] if j == 0 else []
                            mts = S_.do("pe", lambda e, kt=kt, j=j, par=par, hh=hh,
                                        qg=qg: e.matmul(
                                psum_s[par][:, j * 512:(j + 1) * 512],
                                ktil[hh][:, kt * 128:(kt + 1) * 128],
                                qtil[hh][:, qg * 512:(qg + 1) * 512],
                                start=True, stop=True), wait=w, tok=(j == len(kts) - 1))
                        t_e = S_.do("act", lambda e, par=par, n=len(kts): e.activation(
                            pt[:, par * 1024:par * 1024 + n * 512],
                            psum_s[par][:, 0:n * 512], AF.Exp),
                            wait=[mts, pt_guard[par]], tok=True)
                        s_guard[par] = t_e
                        t_mask = t_e
                        for j, kt in enumerate(kts):
                            if kt >= 4 * qg:
                                t_mask = S_.do("pool", lambda e, j=j, kt=kt, par=par,
                                               qg=qg: e.affine_select(
                                    pt[:, par * 1024 + j * 512: par * 1024 + (j + 1) * 512],
                                    pt[:, par * 1024 + j * 512: par * 1024 + (j + 1) * 512],
                                    [[1, 512]], ALU.is_ge, 0.0,
                                    base=qg * 512 - kt * 128, channel_multiplier=-1),
                                    wait=[t_e], tok=True)
                        av = None
                        for j, kt in enumerate(kts):
                            w = [t_mask]
                            if g == 0 and j == 0:
                                w.append(av_guard[0])
                            av = S_.do("pe", lambda e, kt=kt, j=j, par=par, hh=hh: e.matmul(
                                psum_av[:, :], vtil[hh][:, kt * 65:kt * 65 + 65],
                                pt[:, par * 1024 + j * 512: par * 1024 + (j + 1) * 512],
                                start=(kt == 0), stop=(kt == nkts - 1)),
                                wait=w, tok=(j == len(kts) - 1))
                        pt_guard[par] = av
                        last_av = av
                    t_rp = S_.do("dve", lambda e: e.reciprocal(rp[0:1, :], psum_av[64:65, :]),
                                 wait=[last_av], tok=True)
                    t_bc = S_.do("pe", lambda e: e.matmul(
                        psum_bc[:, :], onesrf[0:1, 0:64], rp[0:1, :], start=True, stop=True),
                        wait=[t_rp, t_orf, bc_guard[0]], tok=True)
                    t_bb = S_.do("dve", lambda e: e.tensor_copy(bb[:], psum_bc[:, :]),
                                 wait=[t_bc], tok=True)
                    bc_guard[0] = t_bb
                    t_aT = S_.do("dve", lambda e, hh=hh, qg=qg, pair=pair: e.tensor_tensor(
                        aT[pair][hh * 64:(hh + 1) * 64, qg * 512:(qg + 1) * 512],
                        psum_av[0:64, :], bb[:], ALU.mult), wait=[t_bb], tok=True)
                    av_guard[0] = t_aT
                    aT_last = t_aT
            part_done.append(aT_last)
            flps.close()

        e_stack = ExitStack()
        psum_o = [e_stack.enter_context(nc.psum_tensor(f"psum_o{i}", [128, 512], F32))
                  for i in range(2)]
        xo_guard = [None, None]
        o_guard = [None, None]
        last_store = [None]
        for i in range(NBT):
            p = i % 2
            evs = []
            for dg in range(2):
                mts = None
                for pc in range(2):
                    w = [t_wo, part_done[0], part_done[1], o_guard[dg]] if pc == 0 else []
                    mts = S_.do("pe", lambda e, i=i, dg=dg, pc=pc: e.matmul(
                        psum_o[dg][:, :],
                        aT[pc][:, i * 128:(i + 1) * 128],
                        wo[:, pc * 1024 + dg * 512: pc * 1024 + (dg + 1) * 512],
                        start=(pc == 0), stop=(pc == 1)), wait=w, tok=(pc == 1))
                t_ev = S_.do("dve", lambda e, dg=dg, p=p: e.tensor_copy(
                    xores[p][:, dg * 512:(dg + 1) * 512], psum_o[dg][:, :]),
                    wait=[mts, xo_guard[p]], tok=True)
                o_guard[dg] = t_ev
                evs.append(t_ev)
            t_st = S_.do("sp", lambda e, i=i, p=p: e.dma_start(
                out=part.ap()[i * 128:(i + 1) * 128, :], in_=xores[p][:]),
                wait=evs, tok=True, dma=True)
            xo_guard[p] = t_st
            last_store[0] = t_st
        e_stack.close()
        S_.emit(nc, final_waits=[last_store[0]])
    return nc


def _rope_tables_interleaved(n, nh):
    half = 32
    inv = 1.0 / (10000.0 ** (np.arange(0, HD, 2, dtype=np.float32) / HD))
    ang = np.arange(n, dtype=np.float32)[:, None] * inv[None, :]
    cos = np.cos(ang).astype(np.float32)
    sin = np.sin(ang).astype(np.float32)
    return (np.ascontiguousarray(np.tile(cos, (1, nh))),
            np.ascontiguousarray(np.tile(sin, (1, nh))))


def run_l1a(x, Wq, Wk, Wv, Wo):
    nc = _get("l1a", build_l1a)
    cosi, sini = _rope_tables_interleaved(S, 4)
    in_maps = []
    for c in range(N_CORES):
        b = c // 2 if False else c // 4
        g = c % 4
        in_maps.append({
            "x_b": x[b * S:(b + 1) * S],
            "wq": np.ascontiguousarray(Wq[:, g * 256:(g + 1) * 256]),
            "wk": np.ascontiguousarray(Wk[:, g * 256:(g + 1) * 256]),
            "wv": np.ascontiguousarray(Wv[:, g * 256:(g + 1) * 256]),
            "wo": np.ascontiguousarray(Wo[g * 256:(g + 1) * 256, :]),
            "cosi": cosi, "sini": sini,
        })
    res = run_bass_kernel_spmd(nc, in_maps, core_ids=list(range(N_CORES)))
    partials = []
    for g in range(4):
        partials.append(np.concatenate(
            [res.results[g]["part"], res.results[4 + g]["part"]], axis=0))
    return partials



NLAT = 64
LHD = 64
NLH = 8
LDIM = 512


def build_l1b():
    nc = new_nc("l1b")
    x_my = nc.dram_tensor("x_my", [T, D], F32, kind="ExternalInput")
    x_b = nc.dram_tensor("x_b", [S, D], F32, kind="ExternalInput")
    x_halo = nc.dram_tensor("x_halo", [256, D], F32, kind="ExternalInput")
    hv_d = nc.dram_tensor("hv", [128, 1], F32, kind="ExternalInput")
    wq_d = nc.dram_tensor("wq", [D, D], F32, kind="ExternalInput")
    wk_d = nc.dram_tensor("wk", [D, D], F32, kind="ExternalInput")
    wv_d = nc.dram_tensor("wv", [D, D], F32, kind="ExternalInput")
    wo_d = nc.dram_tensor("wo", [D, D], F32, kind="ExternalInput")
    wgq_d = nc.dram_tensor("wgq", [D, LDIM], F32, kind="ExternalInput")
    wgk_d = nc.dram_tensor("wgk", [D, LDIM], F32, kind="ExternalInput")
    wgv_d = nc.dram_tensor("wgv", [D, LDIM], F32, kind="ExternalInput")
    rout_d = nc.dram_tensor("rout", [LDIM, D], F32, kind="ExternalInput")
    glw_d = nc.dram_tensor("glw", [128, D], F32, kind="ExternalInput")
    grw_d = nc.dram_tensor("grw", [128, D], F32, kind="ExternalInput")
    lmask_d = nc.dram_tensor("lmask", [NLAT, T], F32, kind="ExternalInput")
    xo = nc.dram_tensor("xo", [T, D], F32, kind="ExternalOutput")

    i128c = nc.inline_tensor(np.eye(128, dtype=BF), name="ident128")
    i4c = nc.inline_tensor(np.eye(4, dtype=BF), name="ident4")
    i64c = nc.inline_tensor(np.eye(64, dtype=BF), name="ident64")
    ones1fc = nc.inline_tensor(np.ones((1, 128), dtype=np.float32), name="ones1f")
    ones1bc = nc.inline_tensor(np.ones((1, 128), dtype=BF), name="ones1b")

    S_ = Sched()
    st = ExitStack()
    with st:
        def sb2(stk, name, shape, dt):
            return stk.enter_context(nc.sbuf_tensor(name, shape, dt))

        def sb(name, shape, dt):
            return sb2(st, name, shape, dt)

        idn = sb("idn", [128, 128], BF16)
        idn4 = sb("idn4", [4, 4], BF16)
        idn64b = sb("idn64b", [64, 64], BF16)
        onesrf = sb("onesrf", [1, 128], F32)
        onesrb = sb("onesrb", [1, 128], BF16)
        xTtmp = sb("xTtmp", [128, 1024], BF16)
        wA = sb("wA", [128, 8 * 512], BF16)
        wB = sb("wB", [128, 8 * 512], BF16)
        wC = sb("wC", [128, 8 * 512], BF16)
        wD = sb("wD", [128, 8 * 1024], BF16)
        hv = sb("hv_s", [128, 1], F32)
        glw = sb("glw_s", [128, 1024], F32)
        grw = sb("grw_s", [128, 1024], F32)
        lmask = sb("lmask_s", [64, 2048], F32)
        xt = [sb(f"xt{i}", [128, 1024], F32) for i in range(2)]
        stats = [sb(f"stats{i}", [128, 16], F32) for i in range(2)]
        sqln = sb("sqln", [128, 1024], BF16)
        sqkv = sb("sqkv", [128, 768], BF16)
        xhat = [sb(f"xhat{i}", [128, 1024], BF16) for i in range(2)]
        phT = sb("phT", [128, 8 * 64], BF16)
        ptmp = sb("ptmp", [128, 8], F32)
        gkT = sb("gkT", [128, 4 * 64], BF16)
        gvt = sb("gvt", [64, 8 * 65], BF16)
        comb = sb("comb", [128, 16 * 1024], BF16)
        glall = sb("glall", [128, 256], F32)
        grall = sb("grall", [128, 256], F32)
        n2t = sb("n2t", [128, 4], F32)
        n2tb = sb("n2tb", [128, 4], BF16)
        nm2 = sb("nm2", [4, 1], BF16)
        nkmax = sb("nkmax", [1, 4], F32)
        nkb = sb("nkb", [1, 4], BF16)
        mkb = sb("mkb", [128, 4], F32)
        ctile = sb("ctile", [128, 4], BF16)
        n2q_all = sb("n2q_all", [128, 4 * NMT], F32)
        rp = sb("rp", [1, 512], F32)
        bb = sb("bb", [64, 512], F32)
        anorm = sb("anorm", [64, 128], BF16)
        gstage = sb("gstage", [128, 256], F32)

        psum_tr = st.enter_context(nc.psum_tensor("psum_tr", [128, 1024], BF16))

        t_id = S_.do("sp", lambda e: e.dma_start(out=idn[:], in_=i128c.ap()), tok=True, dma=True)
        t_id2 = S_.do("sp", lambda e: e.dma_start(out=idn4[:], in_=i4c.ap()), tok=True, dma=True)
        t_id64 = S_.do("sp", lambda e: e.dma_start(out=idn64b[:], in_=i64c.ap()), tok=True, dma=True)
        t_orf = S_.do("sp", lambda e: e.dma_start(out=onesrf[:], in_=ones1fc.ap()), tok=True, dma=True)
        t_orb = S_.do("sp", lambda e: e.dma_start(out=onesrb[:], in_=ones1bc.ap()), tok=True, dma=True)
        t_hv = S_.do("sp", lambda e: e.dma_start(out=hv[:], in_=hv_d.ap()), tok=True, dma=True)
        t_glw = S_.do("sp", lambda e: e.dma_start(out=glw[:], in_=glw_d.ap()), tok=True, dma=True)
        t_grw = S_.do("sp", lambda e: e.dma_start(out=grw[:], in_=grw_d.ap()), tok=True, dma=True)
        t_lm = S_.do("sp", lambda e: e.dma_start(out=lmask[:], in_=lmask_d.ap()), tok=True, dma=True)

        def load_w_cols(dst, src_dram, c0, c1, wait=(), ncols=None):
            K = src_dram.shape[0]
            nchunk = K // 128
            n = c1 - c0
            ap = src_dram.ap()[:, c0:c1].rearrange("(c p) n -> p c n", p=128)
            dstap = dst[:, 0:nchunk * n].rearrange("p (c n) -> p c n", c=nchunk)
            return S_.do("pool", lambda e: e.dma_start(out=dstap, in_=ap),
                         wait=wait, tok=True, dma=True)

        def emit_ln(xtb, stat, dst, wait):
            c = lambda k: stat[:, k:k + 1]
            t1 = S_.do("dve", lambda e: e.reduce_sum(c(0), xtb[:], axis=AX.X),
                       wait=wait, tok=True)
            t2 = S_.do("act", lambda e: e.activation(sqln[:], xtb[:], AF.Square,
                                                     accum_out=c(1)), wait=wait, tok=True)
            S_.do("dve", lambda e: e.tensor_scalar(c(2), c(0), 1.0 / D, None, ALU.mult),
                  wait=[t1])
            S_.do("dve", lambda e: e.tensor_scalar(c(3), c(1), 1.0 / D, None, ALU.mult),
                  wait=[t2])
            S_.do("dve", lambda e: e.tensor_tensor(c(4), c(2), c(2), ALU.mult))
            S_.do("dve", lambda e: e.tensor_tensor(c(5), c(3), c(4), ALU.subtract))
            t3 = S_.do("dve", lambda e: e.tensor_scalar(c(5), c(5), EPS_LN, None, ALU.add),
                       tok=True)
            S_.do("act", lambda e: e.activation(c(6), c(5), AF.Ln), wait=[t3])
            t4 = S_.do("act", lambda e: e.activation(c(6), c(6), AF.Exp, scale=-0.5),
                       tok=True)
            t5 = S_.do("dve", lambda e: e.scalar_tensor_tensor(
                c(7), c(2), -1.0, c(6), ALU.mult, ALU.mult), wait=[t4], tok=True)
            t6 = S_.do("act", lambda e: e.activation(
                dst[:], xtb[:], AF.Identity, bias=c(7), scale=c(6)), wait=[t5], tok=True)
            return t6

        tr_guard = [None]

        def emit_transpose_tile(src, wait, dst_emit, nch=8):
            tts = None
            for c in range(nch):
                w = (list(wait) + [tr_guard[0], t_id]) if c == 0 else []
                tts = S_.do("pe", lambda e, c=c, src=src: e.transpose(
                    psum_tr[:, c * 128:(c + 1) * 128], src[:, c * 128:(c + 1) * 128], idn[:]),
                    wait=w, tok=(c == nch - 1))
            t_ev = dst_emit([tts])
            tr_guard[0] = t_ev
            return t_ev

        pa = ExitStack()
        xhTt = [sb2(pa, f"xhTt{i}", [128, 1024], BF16) for i in range(2)]
        xh_guard = [None, None]
        ph_guard = [None]
        for i in range(NBT):
            p = i % 2
            xtb, stat, xh, xT = xt[p], stats[p], xhat[p], xhTt[p]
            t_ld = S_.do("sp", lambda e, i=i, xtb=xtb: e.dma_start(
                out=xtb[:], in_=x_b.ap()[i * 128:(i + 1) * 128, :]),
                wait=[xh_guard[p]], tok=True, dma=True)
            t_ln = emit_ln(xtb, stat, xh, [t_ld])
            t_xT = emit_transpose_tile(xh, [t_ln], lambda wt, xT=xT: S_.do(
                "dve", lambda e: e.tensor_copy(xT[:], psum_tr[:]), wait=wt, tok=True))
            xh_guard[p] = t_xT
            t_pl = S_.do("dve", lambda e, xT=xT: e.tensor_reduce(
                ptmp[:, :].rearrange("p (c o) -> p c o", o=1),
                xT[:, :].rearrange("p (c t) -> p c t", c=8), AX.X, ALU.add),
                wait=[t_xT], tok=True)
            t_pl = S_.do("dve", lambda e, i=i: e.tensor_copy(
                phT[:, :].rearrange("p (c l) -> p c l", l=64)[:, :, i:i + 1],
                ptmp[:, :].rearrange("p (c o) -> p c o", o=1)), tok=True)
            ph_guard[0] = t_pl
        ph_done = ph_guard[0]

        t_wgk = load_w_cols(wD, wgk_d, 0, LDIM)
        pb = ExitStack()
        psum_b = pb.enter_context(nc.psum_tensor("psum_gb", [128, 512], F32))
        gk_guard = [None]
        for gf in range(4):
            mts = None
            for ec in range(8):
                w = [ph_done, t_wgk, gk_guard[0]] if ec == 0 else []
                mts = S_.do("pe", lambda e, gf=gf, ec=ec: e.matmul(
                    psum_b[0:128, gf * 64:(gf + 1) * 64],
                    wD[:, ec * LDIM + gf * 128:ec * LDIM + (gf + 1) * 128],
                    phT[:, ec * 64:(ec + 1) * 64],
                    start=(ec == 0), stop=(ec == 7)), wait=w, tok=(ec == 7))
            t_ev = S_.do("act", lambda e, gf=gf: e.activation(
                gkT[:, gf * 64:(gf + 1) * 64], psum_b[0:128, gf * 64:(gf + 1) * 64],
                AF.Copy, scale=1.0 / (128.0 * 8.0)), wait=[mts], tok=True)
            gk_guard[0] = t_ev
        gk_done = gk_guard[0]
        t_wgv = load_w_cols(wD, wgv_d, 0, LDIM, wait=[mts])
        t_gv1 = S_.do("dve", lambda e: e.memset(
            gvt[:, :].rearrange("p (k c) -> p k c", c=65)[:, :, 64:65], 1.0), tok=True)
        mts2 = None
        for dg in range(1):
            for ec in range(8):
                w = [ph_done, t_wgv, gk_done] if ec == 0 else []
                mts2 = S_.do("pe", lambda e, ec=ec: e.matmul(
                    psum_b[0:64, 0:512], phT[:, ec * 64:(ec + 1) * 64],
                    wD[:, ec * LDIM:(ec + 1) * LDIM],
                    start=(ec == 0), stop=(ec == 7)), wait=w, tok=(ec == 7))
        t_gv = None
        for lh in range(NLH):
            t_gv = S_.do("act", lambda e, lh=lh: e.activation(
                gvt[:, lh * 65:lh * 65 + 64], psum_b[0:64, lh * 64:(lh + 1) * 64],
                AF.Copy, scale=1.0 / 128.0), wait=([mts2, t_gv1] if lh == 0 else []),
                tok=(lh == NLH - 1))
        gv_done = t_gv
        pb.close()
        pa.close()

        xq_stack = ExitStack()
        xT_my = sb2(xq_stack, "xT_my", [128, 8 * 2048], BF16)
        pc = ExitStack()
        qtil = [sb2(pc, f"qtilb{h}", [65, 2048], BF16) for h in range(4)]
        ktil = [sb2(pc, f"ktilb{h}", [65, 2304], BF16) for h in range(4)]
        vtil = [sb2(pc, f"vtilb{h}", [128, 18 * 65], BF16) for h in range(4)]
        qkvs = sb2(pc, "qkvs", [128, 768], BF16)
        ptb = sb2(pc, "ptb", [128, 2 * 384], BF16)
        psc = ExitStack()
        psum_p = psc.enter_context(nc.psum_tensor("psum_pc", [128, 1024], F32))
        psum_h2 = psc.enter_context(nc.psum_tensor("psum_h2c", [4, 128], BF16))
        psum_h1 = psc.enter_context(nc.psum_tensor("psum_h1c", [1, 128], BF16))
        psum_s = psc.enter_context(nc.psum_tensor("psum_sc", [128, 384], F32))
        psum_av = psc.enter_context(nc.psum_tensor("psum_avc", [65, 128], F32))
        psum_bc = psc.enter_context(nc.psum_tensor("psum_bcc", [64, 128], F32))

        statc_done = [None]
        mytiles_done = []
        pn2_guard = [None]
        pn1_guard = [None]
        nm2_guard = [None]
        pr_guard = [None]
        qkv_guard = [None]
        myT_done = [None]
        for g in range(4):
            t_wq = load_w_cols(wA, wq_d, g * 256, (g + 1) * 256, wait=[pr_guard[0]])
            t_wk = load_w_cols(wB, wk_d, g * 256, (g + 1) * 256, wait=[pr_guard[0]])
            t_wv = load_w_cols(wC, wv_d, g * 256, (g + 1) * 256, wait=[pr_guard[0]])
            t_nk0 = S_.do("dve", lambda e: e.memset(nkmax[:], 0.0), tok=True)
            t_k1 = [S_.do("dve", lambda e, h=h: e.memset(ktil[h][64:65, :], 1.0), tok=True)
                    for h in range(4)]
            t_v1 = [S_.do("dve", lambda e, h=h: e.memset(
                vtil[h][:, :].rearrange("p (k c) -> p k c", c=65)[:, :, 64:65], 1.0),
                tok=True) for h in range(4)]
            c_last = None
            for ti in range(18):
                p = ti % 2
                xtb, stat, xh = xt[p], stats[p], xhat[p]
                if ti < 2:
                    src = x_halo.ap()[ti * 128:(ti + 1) * 128, :]
                else:
                    src = x_my.ap()[(ti - 2) * 128:(ti - 1) * 128, :]
                t_ld = S_.do("sp", lambda e, src=src, xtb=xtb: e.dma_start(
                    out=xtb[:], in_=src), wait=[xh_guard[p]], tok=True, dma=True)
                t_ln = emit_ln(xtb, stat, xh, [t_ld])
                if g == 0 and ti >= 2:
                    t_xT = emit_transpose_tile(xh, [t_ln], lambda wt, ti=ti: S_.do(
                        "dve", lambda e, ti=ti: e.tensor_copy(
                            xT_my[:, :].rearrange("p (c t) -> p c t", c=8)
                            [:, :, (ti - 2) * 128:(ti - 1) * 128],
                            psum_tr[:, :].rearrange("p (c t) -> p c t", c=8)),
                        wait=wt, tok=True))
                    myT_done[0] = t_xT
                    xh_guard[p] = t_xT
                    xT_src = None
                else:
                    xT_src = None
                tr_tok = emit_transpose_tile(xh, [t_ln], lambda wt: S_.do(
                    "dve", lambda e: e.tensor_copy(xTtmp[:], psum_tr[:]),
                    wait=wt, tok=True)) if (g > 0 or ti < 2) else None
                if tr_tok is None:
                    def xTs(ec, ti=ti):
                        return xT_my[:, ec * 2048 + (ti - 2) * 128: ec * 2048 + (ti - 1) * 128]
                    dep = myT_done[0]
                else:
                    def xTs(ec):
                        return xTtmp[:, ec * 128:(ec + 1) * 128]
                    dep = tr_tok
                    xh_guard[p] = tr_tok
                mts = None
                for gi, (wbuf, wtk) in enumerate(((wA, t_wq), (wB, t_wk), (wC, t_wv))):
                    for ec in range(8):
                        w = [dep, wtk, pr_guard[0]] if ec == 0 else []
                        mts = S_.do("pe", lambda e, gi=gi, ec=ec, wbuf=wbuf,
                                    xTs=xTs: e.matmul(
                            psum_p[:, gi * 256:(gi + 1) * 256],
                            xTs(ec), wbuf[:, ec * 256:(ec + 1) * 256],
                            start=(ec == 0), stop=(ec == 7)), wait=w, tok=(ec == 7))
                t_qkv = S_.do("dve", lambda e: e.tensor_copy(qkvs[:], psum_p[:, 0:768]),
                              wait=[mts, qkv_guard[0]], tok=True)
                pr_guard[0] = t_qkv
                t_ks = S_.do("dve", lambda e: e.tensor_scalar(
                    qkvs[:, 256:512], qkvs[:, 256:512], 0.125, None, ALU.mult), tok=True)
                if ti >= 2:
                    S_.do("dve", lambda e, g=g: e.tensor_tensor(
                        gstage[:, 0:256], qkvs[:, 0:256],
                        glw[:, g * 256:(g + 1) * 256], ALU.mult), wait=[t_glw])
                    S_.do("dve", lambda e, g=g, ti=ti: e.tensor_reduce(
                        glall[:, :].rearrange("p (h t) -> p h t", t=16)
                        [:, g * 4:(g + 1) * 4, ti - 2:ti - 1],
                        gstage[:, 0:256].rearrange("p (h k) -> p h k", k=64),
                        AX.X, ALU.add))
                    S_.do("dve", lambda e, g=g: e.tensor_tensor(
                        gstage[:, 0:256], qkvs[:, 0:256],
                        grw[:, g * 256:(g + 1) * 256], ALU.mult), wait=[t_glw, t_grw])
                    S_.do("dve", lambda e, g=g, ti=ti: e.tensor_reduce(
                        grall[:, :].rearrange("p (h t) -> p h t", t=16)
                        [:, g * 4:(g + 1) * 4, ti - 2:ti - 1],
                        gstage[:, 0:256].rearrange("p (h k) -> p h k", k=64),
                        AX.X, ALU.add))
                t_sq = S_.do("dve", lambda e: e.tensor_tensor(
                    sqkv[:, 0:256], qkvs[:, 256:512], qkvs[:, 256:512], ALU.mult),
                    wait=[t_ks], tok=True)
                t_n2 = S_.do("dve", lambda e: e.tensor_reduce(
                    n2t[:], sqkv[:, 0:256].rearrange("p (h k) -> p h k", k=64),
                    AX.X, ALU.add), tok=True)
                t_n2b = S_.do("dve", lambda e: e.tensor_copy(n2tb[:], n2t[:]), tok=True)
                t_t1 = S_.do("pe", lambda e: e.transpose(psum_h2[:, :], n2tb[:], idn[:]),
                             wait=[t_n2b, pn2_guard[0], t_id], tok=True)
                t_r = S_.do("dve", lambda e: e.reduce_max(nm2[:, 0:1], psum_h2[:, :],
                                                          axis=AX.X),
                            wait=[t_t1, nm2_guard[0]], tok=True)
                pn2_guard[0] = t_r
                t_t2 = S_.do("pe", lambda e: e.transpose(
                    psum_h1[0:1, 0:4], nm2[:, 0:1], idn4[:]),
                    wait=[t_r, pn1_guard[0], t_id2], tok=True)
                nm2_guard[0] = t_t2
                t_mx = S_.do("dve", lambda e: e.tensor_tensor(
                    nkmax[:], nkmax[:], psum_h1[0:1, 0:4], ALU.max),
                    wait=[t_t2, t_nk0], tok=True)
                pn1_guard[0] = t_mx
                if ti >= 2:
                    t_sq2 = S_.do("dve", lambda e: e.tensor_tensor(
                        sqkv[:, 0:256], qkvs[:, 0:256], qkvs[:, 0:256], ALU.mult),
                        tok=True)
                    t_n2q = S_.do("dve", lambda e, ti=ti: e.tensor_reduce(
                        n2q_all[:, (ti - 2) * 4:(ti - 1) * 4],
                        sqkv[:, 0:256].rearrange("p (h k) -> p h k", k=64),
                        AX.X, ALU.add), tok=True)
                def dst_qk(wt, ti=ti):
                    last = None
                    for hh in range(4):
                        cc = hh // 2
                        rr = (hh % 2) * 64
                        if ti >= 2:
                            S_.do("dve", lambda e, hh=hh, cc=cc, rr=rr, ti=ti: e.tensor_copy(
                                qtil[hh][0:64, (ti - 2) * 128:(ti - 1) * 128],
                                psum_tr[rr:rr + 64, cc * 128:(cc + 1) * 128]),
                                wait=(wt if hh == 0 else []))
                        last = S_.do("dve", lambda e, hh=hh, cc=cc, rr=rr, ti=ti: e.tensor_copy(
                            ktil[hh][0:64, ti * 128:(ti + 1) * 128],
                            psum_tr[rr:rr + 64, 256 + cc * 128:256 + (cc + 1) * 128]),
                            wait=(wt if (hh == 0 and ti < 2) else []), tok=(hh == 3))
                    return last
                tts = None
                for cc in range(4):
                    w = [t_qkv, t_ks, t_sq, tr_guard[0]] + t_k1 if cc == 0 else []
                    tts = S_.do("pe", lambda e, cc=cc: e.transpose(
                        psum_tr[:, cc * 128:(cc + 1) * 128],
                        qkvs[:, cc * 128:(cc + 1) * 128], idn[:]), wait=w, tok=(cc == 3))
                t_qk = dst_qk([tts])
                tr_guard[0] = t_qk
                t_vc = None
                for hh in range(4):
                    t_vc = S_.do("dve", lambda e, hh=hh, ti=ti: e.tensor_copy(
                        vtil[hh][:, ti * 65:ti * 65 + 64],
                        qkvs[:, 512 + hh * 64:512 + (hh + 1) * 64]),
                        wait=([t_v1[hh]] if ti == 0 else []), tok=(hh == 3))
                qkv_guard[0] = t_vc
            t_nkb = S_.do("dve", lambda e: e.tensor_copy(nkb[:], nkmax[:]),
                          wait=[pn1_guard[0]], tok=True)
            t_mkbm = S_.do("pe", lambda e: e.matmul(psum_p[:, 768:772], onesrb[:, :],
                                                    nkb[0:1, :], start=True, stop=True),
                           wait=[t_nkb, t_orb, pr_guard[0]], tok=True)
            t_mkb = S_.do("dve", lambda e: e.tensor_copy(mkb[:], psum_p[:, 768:772]),
                          wait=[t_mkbm], tok=True)
            for ti in range(NMT):
                t_c1 = S_.do("dve", lambda e, ti=ti: e.tensor_tensor(
                    ctile[:, 0:4], n2q_all[:, ti * 4:(ti + 1) * 4], mkb[:], ALU.mult),
                    wait=[t_mkb], tok=True)
                t_c2 = S_.do("act", lambda e: e.activation(ctile[:], ctile[:], AF.Sqrt,
                                                           scale=1.1025),
                             wait=[t_c1], tok=True)
                for hh in range(4):
                    t_tc = S_.do("pe", lambda e, hh=hh: e.transpose(
                        psum_h1[0:1, :], ctile[:, hh:hh + 1], idn[:]),
                        wait=[t_c2, pn1_guard[0], t_id], tok=True)
                    t_wr = S_.do("dve", lambda e, hh=hh, ti=ti: e.tensor_scalar(
                        qtil[hh][64:65, ti * 128:(ti + 1) * 128], psum_h1[0:1, :],
                        -1.0, None, ALU.mult), wait=[t_tc], tok=True)
                    pn1_guard[0] = t_wr
            q_done = pn1_guard[0]
            t_sgl = S_.do("act", lambda e, g=g: e.activation(
                glall[:, g * 64:(g + 1) * 64], glall[:, g * 64:(g + 1) * 64], AF.Sigmoid),
                wait=[q_done], tok=True)
            t_sgr = S_.do("act", lambda e, g=g: e.activation(
                grall[:, g * 64:(g + 1) * 64], grall[:, g * 64:(g + 1) * 64], AF.Sigmoid),
                tok=True)
            s_guard = [None]
            pt_guard = [None, None]
            av_guard = [q_done]
            bc_guard = [None]
            an_guard = [None]
            cb_guard = [None]
            for hh in range(4):
                h = g * 4 + hh
                for qt in range(NMT):
                    kts = [qt, qt + 1, qt + 2]
                    mts = None
                    for j, kt in enumerate(kts):
                        w = [q_done, s_guard[0], t_sgr] if j == 0 else []
                        mts = S_.do("pe", lambda e, kt=kt, j=j, hh=hh, qt=qt: e.matmul(
                            psum_s[:, j * 128:(j + 1) * 128],
                            ktil[hh][:, kt * 128:(kt + 1) * 128],
                            qtil[hh][:, qt * 128:(qt + 1) * 128],
                            start=True, stop=True), wait=w, tok=(j == 2))
                    par = qt % 2
                    t_e = S_.do("act", lambda e, par=par: e.activation(
                        ptb[:, par * 384:(par + 1) * 384], psum_s[:, 0:384], AF.Exp),
                        wait=[mts, pt_guard[par]], tok=True)
                    s_guard[0] = t_e
                    t_m1 = S_.do("pool", lambda e, par=par: e.affine_select(
                        ptb[:, par * 384:par * 384 + 128],
                        ptb[:, par * 384:par * 384 + 128],
                        [[-1, 128]], ALU.is_ge, 0.0, base=-1, channel_multiplier=1),
                        wait=[t_e], tok=True)
                    t_m2 = S_.do("pool", lambda e, par=par: e.affine_select(
                        ptb[:, par * 384 + 256:par * 384 + 384],
                        ptb[:, par * 384 + 256:par * 384 + 384],
                        [[1, 128]], ALU.is_ge, 0.0, base=0, channel_multiplier=-1),
                        tok=True)
                    t_mask = t_m2
                    if qt < 2:
                        nh_ = 2 - qt
                        t_mask = S_.do("dve", lambda e, par=par, nh_=nh_: e.tensor_scalar(
                            ptb[:, par * 384:par * 384 + nh_ * 128],
                            ptb[:, par * 384:par * 384 + nh_ * 128],
                            hv[:, 0:1], None, ALU.mult),
                            wait=[t_m1, t_m2, t_hv], tok=True)
                    av = None
                    for j, kt in enumerate(kts):
                        w = [t_mask]
                        if j == 0:
                            w.append(av_guard[0])
                        av = S_.do("pe", lambda e, kt=kt, j=j, par=par, hh=hh: e.matmul(
                            psum_av[:, :], vtil[hh][:, kt * 65:kt * 65 + 65],
                            ptb[:, par * 384 + j * 128:par * 384 + (j + 1) * 128],
                            start=(j == 0), stop=(j == 2)), wait=w, tok=(j == 2))
                    pt_guard[par] = av
                    t_rp = S_.do("dve", lambda e: e.reciprocal(rp[0:1, 0:128],
                                                               psum_av[64:65, :]),
                                 wait=[av], tok=True)
                    t_bc = S_.do("pe", lambda e: e.matmul(
                        psum_bc[:, :], onesrf[0:1, 0:64], rp[0:1, 0:128],
                        start=True, stop=True), wait=[t_rp, t_orf, bc_guard[0]], tok=True)
                    t_bb = S_.do("dve", lambda e: e.tensor_copy(bb[:, 0:128], psum_bc[:, :]),
                                 wait=[t_bc], tok=True)
                    bc_guard[0] = t_bb
                    t_an = S_.do("dve", lambda e: e.tensor_tensor(
                        anorm[:, :], psum_av[0:64, :], bb[:, 0:128], ALU.mult),
                        wait=[t_bb, an_guard[0]], tok=True)
                    av_guard[0] = t_an
                    t_at = S_.do("pe", lambda e: e.transpose(
                        psum_tr[:, 0:64], anorm[:, :], idn64b[:]),
                        wait=[t_an, tr_guard[0], t_id64], tok=True)
                    t_cb = S_.do("dve", lambda e, h=h, qt=qt: e.tensor_scalar(
                        comb[:, qt * 1024 + h * 64: qt * 1024 + (h + 1) * 64],
                        psum_tr[:, 0:64],
                        glall[:, h * 16 + qt:h * 16 + qt + 1], None, ALU.mult),
                        wait=[t_at, t_sgl], tok=True)
                    tr_guard[0] = t_cb
                    an_guard[0] = t_cb
            pr_guard[0] = q_done
            local_done = [an_guard[0]]
        psc.close()
        pc.close()

        pd = ExitStack()
        GQT = sb2(pd, "GQT", [128, 4 * 2048], BF16)
        remT = sb2(pd, "remT", [128, 4 * 2048], BF16)
        ptl = sb2(pd, "ptl", [64, 512], BF16)
        wD2 = sb2(pd, "wD2", [128, 4 * 1024], BF16)
        pdps = ExitStack()
        psum_gq = pdps.enter_context(nc.psum_tensor("psum_gq", [128, 512], F32))
        psum_l = pdps.enter_context(nc.psum_tensor("psum_l", [64, 512], F32))
        psum_lv = pdps.enter_context(nc.psum_tensor("psum_lv", [65, 512], F32))
        psum_lbc = pdps.enter_context(nc.psum_tensor("psum_lbc", [64, 512], F32))
        psum_o2 = pdps.enter_context(nc.psum_tensor("psum_o2", [128, 1024], F32))

        t_wgq = load_w_cols(wD, wgq_d, 0, LDIM, wait=[pr_guard[0], local_done[0]])
        gq_guard = [None]
        gq_done = [None]
        for gf in range(4):
            for tg in range(4):
                mts = None
                for ec in range(8):
                    w = [t_wgq, myT_done[0], gq_guard[0], local_done[0]] if ec == 0 else []
                    mts = S_.do("pe", lambda e, gf=gf, tg=tg, ec=ec: e.matmul(
                        psum_gq[:, 0:512],
                        wD[:, ec * LDIM + gf * 128:ec * LDIM + (gf + 1) * 128],
                        xT_my[:, ec * 2048 + tg * 512:ec * 2048 + (tg + 1) * 512],
                        start=(ec == 0), stop=(ec == 7)), wait=w, tok=(ec == 7))
                t_ev = S_.do("dve", lambda e, gf=gf, tg=tg: e.tensor_copy(
                    GQT[:, gf * 2048 + tg * 512: gf * 2048 + (tg + 1) * 512],
                    psum_gq[:, 0:512]), wait=[mts], tok=True)
                gq_guard[0] = t_ev
        gq_done[0] = gq_guard[0]

        lm_v = lmask[:, :]
        s_guard = [None]
        pt_guard = [None]
        av_guard = [None]
        bc_guard = [None]
        rT_guard = [None]
        for lh in range(NLH):
            rr = (lh % 2) * 64
            cc = lh // 2
            for qg in range(4):
                t_sc = S_.do("pe", lambda e, lh=lh, qg=qg, rr=rr, cc=cc: e.matmul(
                    psum_l[:, :], gkT[rr:rr + 64, cc * 64:(cc + 1) * 64],
                    GQT[rr:rr + 64, cc * 2048 + qg * 512: cc * 2048 + (qg + 1) * 512],
                    start=True, stop=True),
                    wait=[gq_done[0], gk_done, s_guard[0]], tok=True)
                t_e = S_.do("act", lambda e: e.activation(ptl[:, :], psum_l[:, :], AF.Exp),
                            wait=[t_sc, pt_guard[0]], tok=True)
                s_guard[0] = t_e
                t_mk = S_.do("dve", lambda e, qg=qg: e.tensor_tensor(
                    ptl[:, :], ptl[:, :], lmask[:, qg * 512:(qg + 1) * 512], ALU.mult),
                    wait=[t_e, t_lm], tok=True)
                t_av = S_.do("pe", lambda e, lh=lh: e.matmul(
                    psum_lv[:, :], gvt[:, lh * 65:lh * 65 + 65], ptl[:, :],
                    start=True, stop=True), wait=[t_mk, gv_done, av_guard[0]], tok=True)
                pt_guard[0] = t_av
                t_rp = S_.do("dve", lambda e: e.reciprocal(rp[0:1, :], psum_lv[64:65, :]),
                             wait=[t_av], tok=True)
                t_bc = S_.do("pe", lambda e: e.matmul(
                    psum_lbc[:, :], onesrf[0:1, 0:64], rp[0:1, :], start=True, stop=True),
                    wait=[t_rp, t_orf, bc_guard[0]], tok=True)
                t_bb = S_.do("dve", lambda e: e.tensor_copy(bb[:, :], psum_lbc[:, :]),
                             wait=[t_bc], tok=True)
                bc_guard[0] = t_bb
                t_rT = S_.do("dve", lambda e, rr=rr, cc=cc, qg=qg: e.tensor_tensor(
                    remT[rr:rr + 64, cc * 2048 + qg * 512: cc * 2048 + (qg + 1) * 512],
                    psum_lv[0:64, :], bb[:, :], ALU.mult), wait=[t_bb], tok=True)
                av_guard[0] = t_rT
                rT_guard[0] = t_rT
        rem_done = rT_guard[0]

        t_ro = load_w_cols(wD2, rout_d, 0, D, wait=[t_wgq])
        cb_guard = [None]
        comb_done = [None]
        for ti in range(NMT):
            mts = [None, None]
            for dg in range(2):
                mm = None
                for c4 in range(4):
                    w = [rem_done, t_ro, cb_guard[0]] if (dg == 0 and c4 == 0) else []
                    mm = S_.do("pe", lambda e, ti=ti, dg=dg, c4=c4: e.matmul(
                        psum_o2[:, dg * 512:(dg + 1) * 512],
                        remT[:, c4 * 2048 + ti * 128: c4 * 2048 + (ti + 1) * 128],
                        wD2[:, c4 * 1024 + dg * 512: c4 * 1024 + (dg + 1) * 512],
                        start=(c4 == 0), stop=(c4 == 3)), wait=w, tok=(c4 == 3))
                mts[dg] = mm
            last = None
            for h in range(16):
                dg = h // 8
                w = [mts[dg]]
                last = S_.do("dve", lambda e, ti=ti, h=h: e.scalar_tensor_tensor(
                    comb[:, ti * 1024 + h * 64: ti * 1024 + (h + 1) * 64],
                    psum_o2[:, h * 64:(h + 1) * 64],
                    grall[:, h * 16 + ti:h * 16 + ti + 1],
                    comb[:, ti * 1024 + h * 64: ti * 1024 + (h + 1) * 64],
                    ALU.mult, ALU.add), wait=w, tok=True)
            cb_guard[0] = last
            comb_done[0] = last
        pdps.close()
        pd.close()
        xq_stack.close()

        pe_ = ExitStack()
        aT2 = sb2(pe_, "aT2", [128, 8 * 2048], BF16)
        xores = [sb2(pe_, f"xoresb{i}", [128, 1024], F32) for i in range(2)]
        peps = ExitStack()
        psum_o = [peps.enter_context(nc.psum_tensor(f"psum_oe{i}", [128, 512], F32))
                  for i in range(2)]
        t_wo = load_w_cols(wD, wo_d, 0, D, wait=[gq_done[0]])
        aT_guard = [None]
        for ti in range(NMT):
            t_aT = emit_transpose_tile(
                comb[:, ti * 1024:(ti + 1) * 1024], [comb_done[0]],
                lambda wt, ti=ti: S_.do("dve", lambda e, ti=ti: e.tensor_copy(
                    aT2[:, :].rearrange("p (c t) -> p c t", c=8)[:, :, ti * 128:(ti + 1) * 128],
                    psum_tr[:, :].rearrange("p (c t) -> p c t", c=8)), wait=wt, tok=True))
            aT_guard[0] = t_aT
        xo_guard = [None, None]
        o_guard = [None, None]
        last_store = [None]
        for i in range(NMT):
            p = i % 2
            t_ld = S_.do("sp", lambda e, i=i, p=p: e.dma_start(
                out=xt[p][:], in_=x_my.ap()[i * 128:(i + 1) * 128, :]),
                wait=[xh_guard[p], xo_guard[p]], tok=True, dma=True)
            evs = []
            for dg in range(2):
                mts = None
                for c in range(8):
                    w = [t_wo, aT_guard[0], o_guard[dg]] if c == 0 else []
                    mts = S_.do("pe", lambda e, i=i, dg=dg, c=c: e.matmul(
                        psum_o[dg][:, :],
                        aT2[:, c * 2048 + i * 128: c * 2048 + (i + 1) * 128],
                        wD[:, c * 1024 + dg * 512: c * 1024 + (dg + 1) * 512],
                        start=(c == 0), stop=(c == 7)), wait=w, tok=(c == 7))
                t_ev = S_.do("dve", lambda e, dg=dg, p=p: e.tensor_tensor(
                    xores[p][:, dg * 512:(dg + 1) * 512], psum_o[dg][:, :],
                    xt[p][:, dg * 512:(dg + 1) * 512], ALU.add),
                    wait=[mts, t_ld, xo_guard[p]], tok=True)
                o_guard[dg] = t_ev
                evs.append(t_ev)
            t_st = S_.do("sp", lambda e, i=i, p=p: e.dma_start(
                out=xo.ap()[i * 128:(i + 1) * 128, :], in_=xores[p][:]),
                wait=evs, tok=True, dma=True)
            xo_guard[p] = t_st
            last_store[0] = t_st
        peps.close()
        pe_.close()
        S_.emit(nc, final_waits=[last_store[0]])
    return nc


def run_l1b(x, wq, wk, wv, wgq, wgk, wgv, r_out_w, glw, grw, w_out):
    nc = _get("l1b", build_l1b)
    in_maps = []
    glwr = np.ascontiguousarray(np.tile(glw.reshape(1, D), (128, 1)).astype(np.float32))
    grwr = np.ascontiguousarray(np.tile(grw.reshape(1, D), (128, 1)).astype(np.float32))
    for c in range(N_CORES):
        b = c // 4
        off = c * T
        halo = np.zeros((256, D), np.float32) if c % 4 == 0 else \
            np.ascontiguousarray(x[off - 256:off])
        hv = np.full((128, 1), 0.0 if c % 4 == 0 else 1.0, np.float32)
        gpos = (np.arange(T) + (c % 4) * T) // 128
        lmask = (np.arange(NLAT)[:, None] <= gpos[None, :]).astype(np.float32)
        in_maps.append({
            "x_my": x[off:off + T],
            "x_b": x[b * S:(b + 1) * S],
            "x_halo": halo, "hv": hv,
            "wq": wq, "wk": wk, "wv": wv, "wo": w_out,
            "wgq": wgq, "wgk": wgk, "wgv": wgv, "rout": r_out_w,
            "glw": glwr, "grw": grwr,
            "lmask": np.ascontiguousarray(lmask),
        })
    res = run_bass_kernel_spmd(nc, in_maps, core_ids=list(range(N_CORES)))
    return np.concatenate([res.results[c]["xo"] for c in range(N_CORES)], axis=0)
